# version 13
# baseline (speedup 1.0000x reference)
import numpy as np
import ml_dtypes
from contextlib import ExitStack

import concourse.bass as bass
import concourse.mybir as mybir
from concourse.bass_utils import run_bass_kernel_spmd

F32 = mybir.dt.float32
BF16 = mybir.dt.bfloat16
AF = mybir.ActivationFunctionType
ALU = mybir.AluOpType
AX = mybir.AxisListType

NCORES = 8
D = 1024
NH = 16
DH = 64
HPC = NH // NCORES
E = 15
HF = 1024
KC = 128
QB = 512
DC = D // 128
HC = HF // 128
RPQ = QB // KC
LN4 = 1.3862943611198906


import os as _os0
KSAFE = _os0.environ.get("KSAFE", "0") == "1"
KSAFE_PH = set(_os0.environ.get("KSAFE_PH", "").split(",")) - {""}
_PHASE_MAP = {}
_DESC_MAP = {}


class Prog:

    def __init__(self):
        self.ops = []
        self.cnt = {}
        self.pending = None
        self.stamped = set()
        self.phase = "p0"

    def v(self, sem):
        return self.cnt.get(sem, 0)

    def barrier(self):
        self.pending = {s: self.cnt.get(s, 0)
                        for s in ("te", "act", "dve", "dsy", "dgp", "dsa", "dsb",
                                  "dg0", "dg1", "dg2", "dg3", "dya", "dyb",
                                  "ds0", "ds1", "wr0", "wr1", "wr2")}
        self.stamped = set()

    def add(self, eng, fn, waits=(), inc=None, inc_n=1):
        waits = list(waits)
        if KSAFE or self.phase in KSAFE_PH:
            waits += [(sname, self.cnt.get(sname, 0))
                      for sname in ("te", "act", "dve", "dsy", "dgp", "dsa", "dsb",
                                    "dg0", "dg1", "dg2", "dg3", "dya", "dyb",
                                    "ds0", "ds1", "wr0", "wr1", "wr2")]
        if self.pending is not None and eng not in self.stamped:
            waits += [(s, v) for s, v in self.pending.items() if v > 0]
            self.stamped.add(eng)
        if inc is not None:
            self.cnt[inc] = self.cnt.get(inc, 0) + (1 if inc_n is None else inc_n)
        self.ops.append((eng, fn, tuple(waits), inc, inc_n, self.phase))
        return self.cnt.get(inc, 0) if inc is not None else None

    def emit(self, name, e, sems):
        last = {}
        for op_rec in self.ops:
            eng_name, fn, waits, inc, inc_n = op_rec[:5]
            if eng_name != name:
                continue
            for s, val in waits:
                if val is None or val <= 0:
                    continue
                if last.get(s, -1) >= val:
                    continue
                e.wait_ge(sems[s], val)
                last[s] = val
            ins = fn(e)
            try:
                _PHASE_MAP[ins.ins.name] = op_rec[5]
                _DESC_MAP[ins.ins.name] = (name, waits, inc)
            except Exception:
                pass
            if inc is not None:
                if inc_n is None:
                    ins.then_inc(sems[inc])
                else:
                    ins.then_inc(sems[inc], inc_n)


def build_nc(S, dbg=False, phases="all"):
    B = 2
    N = B * S
    TOWN = N // NCORES
    OTILE = TOWN // 128
    NQB = S // QB
    TB = min(512, TOWN)
    NTB = TOWN // TB
    TGRP = 2 if NTB % 2 == 0 else 1
    NGRP = NTB // TGRP
    HTOK = TGRP * TB
    NB6 = TOWN // TB

    import os as _os2
    nc = bass.Bass("TRN2", target_bir_lowering=False,
                   detect_race_conditions=_os2.environ.get("KRACE", "0") == "1")

    def inp(name, shape, dtype=F32):
        return nc.dram_tensor(name, shape, dtype, kind="ExternalInput").ap()

    x_own = inp("x_own", [TOWN, D])
    x_full = inp("x_full", [N, D])
    wq_my = inp("wq_my", [128, D])
    wkv_i = inp("wkv", [2 * DH, D])
    wout_i = inp("wout", [D, D])
    g1_c = inp("g1_c", [128, DC])
    g2_r = inp("g2_r", [1, D])
    rw_c = inp("rw_c", [128, DC * E])
    rb_r = inp("rb_r", [1, E])
    sw1_i = inp("sw1", [D, HF])
    sw2_i = inp("sw2", [HF, D])
    sb1_c = inp("sb1_c", [128, HC])
    sb2_c = inp("sb2_c", [128, DC])
    ew1_i = inp("ew1", [E, D, HF])
    ew2_i = inp("ew2", [E, HF, D])
    eb1_c = inp("eb1_c", [128, E * HC])
    eb2_i = inp("eb2_t", [E, D])
    eye_bi = inp("eye_b", [128, 128], BF16)
    eye_fi = inp("eye_f", [128, 128], F32)
    ones_i = inp("ones_f", [128, 128], F32)
    qss_i = inp("qss_t", [128, 2], BF16)
    bc2_i = inp("bc2_t", [2, 128], F32)
    ohot_i = inp("ohot_i", [E, E * 128], BF16)
    mask_i = inp("masks", [128, RPQ * QB], BF16)

    y_own = nc.dram_tensor("y_own", [TOWN, D], F32, kind="ExternalOutput").ap()

    cc_in = nc.dram_tensor("cc_in", [128, N], BF16)
    s_dram = nc.dram_tensor("s_dram", [OTILE, 128, D], BF16)
    cc_out = nc.dram_tensor("cc_out", [NCORES * 128, N], BF16)
    if dbg:
        dbg_k = nc.dram_tensor("dbg_k", [128, N], BF16, kind="ExternalOutput").ap()
        dbg_q = nc.dram_tensor("dbg_q", [128, N], BF16, kind="ExternalOutput").ap()
        dbg_v = nc.dram_tensor("dbg_v", [128, (N // KC) * (DH + 1)], BF16, kind="ExternalOutput").ap()
        dbg_xnT = nc.dram_tensor("dbg_xnT", [128, DC * TOWN], BF16, kind="ExternalOutput").ap()
        dbg_cw = nc.dram_tensor("dbg_cw", [128, (TOWN // 128) * 16], F32, kind="ExternalOutput").ap()
        dbg_att = nc.dram_tensor("dbg_att", [128, N], BF16, kind="ExternalOutput").ap()
        dbg_afull = nc.dram_tensor("dbg_afull", [128, DC * TOWN], BF16, kind="ExternalOutput").ap()
        dbg_g = nc.dram_tensor("dbg_g", [128, DC * TOWN], F32, kind="ExternalOutput").ap()
        dbg_s = nc.dram_tensor("dbg_s", [128, (TOWN // 128) * D], BF16, kind="ExternalOutput").ap()
        dbg_o2 = nc.dram_tensor("dbg_o2", [128, DC * 512], BF16, kind="ExternalOutput").ap()
        dbg_g2bc = nc.dram_tensor("dbg_g2bc", [128, D], F32, kind="ExternalOutput").ap()
        dbg_st = nc.dram_tensor("dbg_st", [128, (TOWN // 128) * 8], F32, kind="ExternalOutput").ap()
        dbg_o2t = nc.dram_tensor("dbg_o2t", [128, 4 * D], F32, kind="ExternalOutput").ap()

    import os as _os
    KSKIP = set(_os.environ.get("KSKIP", "").split(","))
    p = Prog()
    W = p.v
    PH = set(phases.split(",")) if phases != "all" else {"p1", "p2", "p3", "cc", "p5", "p6"}
    SEMS = ["dsy", "dgp", "te", "act", "dve", "cc", "dsa", "dsb", "dg0", "dg1", "dg2", "dg3", "dya", "dyb", "ds0", "ds1", "wr0", "wr1", "wr2"]

    with ExitStack() as top:
        T = top.enter_context

        eyeb = T(nc.sbuf_tensor("eyeb", [128, 128], BF16))
        eyef = T(nc.sbuf_tensor("eyef", [128, 128], F32))
        ones = T(nc.sbuf_tensor("ones", [128, 128], F32))
        onesb = T(nc.sbuf_tensor("onesb", [128, 128], BF16))
        qssT = T(nc.sbuf_tensor("qssT", [128, 2], BF16))
        bc2b = T(nc.sbuf_tensor("bc2b", [2, 128], BF16))
        bc2T = T(nc.sbuf_tensor("bc2T", [2, 128], F32))


        sb1c = T(nc.sbuf_tensor("sb1c", [128, HC], F32))
        sb2c = T(nc.sbuf_tensor("sb2c", [128, DC], F32))
        eb1c = T(nc.sbuf_tensor("eb1c", [128, E * HC], F32))
        eb2t = T(nc.sbuf_tensor("eb2t", [E, D], BF16))

        xnT = T(nc.sbuf_tensor("xnT", [128, DC * TOWN], BF16))
        cwb = T(nc.sbuf_tensor("cwb", [128, OTILE * 16], F32))
        cln4 = T(nc.sbuf_tensor("cln4", [128, 1], F32))
        cnhalf = T(nc.sbuf_tensor("cnhalf", [128, 1], F32))
        ceps = T(nc.sbuf_tensor("ceps", [128, 1], F32))

        pre = ExitStack()
        Tp = pre.enter_context
        g1c = Tp(nc.sbuf_tensor("g1c", [128, DC], F32))
        rwc = Tp(nc.sbuf_tensor("rwc", [128, DC * E], F32))
        rwch = Tp(nc.sbuf_tensor("rwch", [128, DC * E], BF16))
        rwcl = Tp(nc.sbuf_tensor("rwcl", [128, DC * E], BF16))
        rbbc = Tp(nc.sbuf_tensor("rbbc", [128, E], F32))
        rbrow = Tp(nc.sbuf_tensor("rbrow", [1, E], F32))
        wqT = Tp(nc.sbuf_tensor("wqT", [128, DC * 128], BF16))
        wkvT = Tp(nc.sbuf_tensor("wkvT", [128, DC * 128], BF16))
        sqscr = Tp(nc.sbuf_tensor("sqscr", [128, D], BF16))

        for dst, src in [
            (eyeb[:, :], eye_bi), (eyef[:, :], eye_fi), (ones[:, :], ones_i),
            (qssT[:, :], qss_i), (bc2T[:, :], bc2_i),
            (g1c[:, :], g1_c), (sb1c[:, :], sb1_c), (sb2c[:, :], sb2_c),
            (eb1c[:, :], eb1_c), (rwc[:, :], rw_c),
            (rbrow[:, :], rb_r),
        ]:
            p.add("sync", lambda e, d=dst, s=src: e.dma_start(out=d, in_=s),
                  inc="dsy", inc_n=16)
        ld0 = W("dsy")

        p.add("dve", lambda e: e.tensor_copy(onesb[:, :], ones[:, :]),
              waits=[("dsy", ld0)], inc="dve")
        p.add("dve", lambda e: e.tensor_copy(bc2b[:, :], bc2T[:, :]),
              waits=[("dsy", ld0)], inc="dve")
        p.add("dve", lambda e: e.tensor_copy(rwch[:, :], rwc[:, :]),
              waits=[("dsy", ld0)], inc="dve")
        rwv = W("dve")
        p.add("dve", lambda e: e.scalar_tensor_tensor(
            rwcl[:, :], rwch[:, :], -1.0, rwc[:, :], ALU.mult, ALU.add),
            waits=[("dve", rwv)], inc="dve")
        p.add("dve", lambda e: e.memset(cln4[:, :], LN4), inc="dve")
        p.add("dve", lambda e: e.memset(cwb[:, :], 0.0), inc="dve")
        p.add("dve", lambda e: e.memset(cnhalf[:, :], -0.5), inc="dve")
        p.add("dve", lambda e: e.memset(ceps[:, :], 1e-5), inc="dve")

        with nc.sbuf_tensor("wscr_f", [128, D], F32) as wscr_f, \
             nc.sbuf_tensor("wscr_b", [128, D], BF16) as wscr_b, \
             nc.psum_tensor("ps0a", [128, 512], F32) as ps0a, \
             nc.psum_tensor("ps0b", [128, 1024], BF16) as ps0b:

            if "rbbc" not in KSKIP:
                p.add("dve", lambda e: e.tensor_copy(wscr_b[0:1, 0:E], rbrow[0:1, :]),
                      waits=[("dsy", ld0), ("dve", W("dve"))], inc="dve")
                hv = W("dve")
                p.add("dve", lambda e: e.scalar_tensor_tensor(
                    wscr_b[0:1, 16:16 + E], wscr_b[0:1, 0:E], -1.0, rbrow[0:1, :],
                    ALU.mult, ALU.add), waits=[("dve", hv)], inc="dve")
                lv0 = W("dve")
                p.add("te", lambda e: e.matmul(
                    ps0a[:, 0:E], onesb[0:1, :], wscr_b[0:1, 0:E],
                    start=True, stop=False), waits=[("dve", lv0)], inc="te")
                p.add("te", lambda e: e.matmul(
                    ps0a[:, 0:E], onesb[0:1, :], wscr_b[0:1, 16:16 + E],
                    start=False, stop=True), inc="te")
                p.add("dve", lambda e: e.tensor_copy(rbbc[:, :], ps0a[:, 0:E]),
                      waits=[("te", W("te"))], inc="dve")

            def prep_T(src_ap, nrow, dst, dst_col_fn, scale_fn):
                p.add("sync", lambda e, s=src_ap, n=nrow: e.dma_start(
                    out=wscr_f[0:n, :], in_=s),
                    waits=[("dve", W("dve")), ("act", W("act")), ("te", W("te"))],
                    inc="dsy", inc_n=16)
                lv = W("dsy")
                p.add("dve", lambda e, n=nrow: e.tensor_copy(
                    wscr_b[0:n, :], wscr_f[0:n, :]), waits=[("dsy", lv)], inc="dve")
                cv = W("dve")
                for dc in range(DC):
                    p.add("te", lambda e, dc=dc, n=nrow: e.transpose(
                        ps0b[:, 0:n], wscr_b[0:n, dc * 128:(dc + 1) * 128],
                        eyeb[0:n, 0:n]),
                        waits=[("dve", cv), ("act", W("act"))], inc="te")
                    tv = W("te")
                    sc = scale_fn(dc)
                    if sc is None:
                        p.add("act", lambda e, dc=dc, n=nrow: e.activation(
                            dst[:, dst_col_fn(dc):dst_col_fn(dc) + n],
                            ps0b[:, 0:n], AF.Copy), waits=[("te", tv)], inc="act")
                    else:
                        p.add("act", lambda e, dc=dc, n=nrow, sc=sc: e.activation(
                            dst[:, dst_col_fn(dc):dst_col_fn(dc) + n],
                            ps0b[:, 0:n], AF.Copy, scale=sc),
                            waits=[("te", tv)], inc="act")

            if "eb2" not in KSKIP:
                p.add("sync", lambda e: e.dma_start(out=wscr_f[0:E, :], in_=eb2_i),
                      inc="dsy", inc_n=16)
                eb2ld = W("dsy")
                p.add("dve", lambda e: e.tensor_copy(eb2t[:, :], wscr_f[0:E, :]),
                      waits=[("dsy", eb2ld)], inc="dve")
            if "prept" not in KSKIP:
                prep_T(wq_my, 128, wqT, lambda dc: dc * 128, lambda dc: g1c[:, dc:dc + 1])
                prep_T(wkv_i, 128, wkvT, lambda dc: dc * 128, lambda dc: g1c[:, dc:dc + 1])

        def ln_chain(xt_ap, st, c0, wx):
            s1, ss = st[:, c0:c0 + 1], st[:, c0 + 1:c0 + 2]
            nm, var = st[:, c0 + 2:c0 + 3], st[:, c0 + 3:c0 + 4]
            rstd = st[:, c0 + 4:c0 + 5]
            p.add("dve", lambda e: e.reduce_sum(s1, xt_ap, axis=AX.X),
                  waits=list(wx), inc="dve")
            rv = W("dve")
            p.add("act", lambda e: e.activation(sqscr[:, :], xt_ap, AF.Square,
                                                accum_out=ss),
                  waits=list(wx), inc="act")
            qv = W("act")
            p.add("dve", lambda e: e.tensor_scalar(nm, s1, -1.0 / D, None, ALU.mult),
                  waits=[("dve", rv)], inc="dve")
            nv = W("dve")
            p.add("dve", lambda e: e.tensor_tensor(s1, nm, nm, ALU.mult),
                  waits=[("dve", nv)], inc="dve")
            mv = W("dve")
            p.add("dve", lambda e: e.tensor_scalar(var, ss, 1.0 / D, None, ALU.mult),
                  waits=[("act", qv), ("dve", mv)], inc="dve")
            vv = W("dve")
            p.add("dve", lambda e: e.tensor_tensor(var, var, s1, ALU.subtract),
                  waits=[("dve", vv)], inc="dve")
            vv2 = W("dve")
            p.add("act", lambda e: e.activation(rstd, var, AF.Ln, bias=ceps[:, 0:1]),
                  waits=[("dve", vv2)], inc="act")
            lv = W("act")
            p.add("act", lambda e: e.activation(rstd, rstd, AF.Exp,
                                                scale=cnhalf[:, 0:1]),
                  waits=[("act", lv)], inc="act")
            return W("act"), nv

        with ExitStack() as sc_pa:
            Tp = sc_pa.enter_context
            p.barrier()
            maskb = Tp(nc.sbuf_tensor("maskb", [128, RPQ * QB], BF16))
            p.add("sync", lambda e: e.dma_start(out=maskb[:, :], in_=mask_i),
                  inc="dsy", inc_n=16)
            kbuf = Tp(nc.sbuf_tensor("kbuf", [128, N], BF16))
            vbuf = Tp(nc.sbuf_tensor("vbuf", [128, (N // KC) * (DH + 1)], BF16))
            qhT = Tp(nc.sbuf_tensor("qhT", [128, N], BF16))

            with ExitStack() as sc_p:
                T1 = sc_p.enter_context

                with ExitStack() as s1s:
                    T1a = s1s.enter_context
                    xt2 = T1a(nc.sbuf_tensor("xt2", [128, 2 * D], F32))
                    xnt = T1a(nc.sbuf_tensor("xnt", [128, D], F32))
                    st8 = T1a(nc.sbuf_tensor("st8", [128, 12], F32))
                    xnh = T1a(nc.sbuf_tensor("xnh", [128, D], BF16))
                    xnl = T1a(nc.sbuf_tensor("xnl", [128, D], BF16))
                    xnTl = T1a(nc.sbuf_tensor("xnTl", [128, D], BF16))
                    lgs = T1a(nc.sbuf_tensor("lgs", [128, 16], F32))
                    lgs2 = T1a(nc.sbuf_tensor("lgs2", [128, 16], F32))
                    msk = T1a(nc.sbuf_tensor("msk", [128, 2 * 16], F32))
                    lgs3 = T1a(nc.sbuf_tensor("lgs3", [128, 4], F32))
                    ps_t = T1a(nc.psum_tensor("ps_t", [128, D], BF16))
                    ps_tl = T1a(nc.psum_tensor("ps_tl", [128, D], BF16))
                    ps_lg = T1a(nc.psum_tensor("ps_lg", [128, 512], F32))

                    xo_t = x_own.rearrange("(t p) d -> t p d", p=128)
                    p.phase = "p1"
                    p.barrier()
                    xgate = {0: [], 1: []}
                    lgv_prev = 0
                    trv_prev = 0
                    for t in range(OTILE if "p1" in PH else 0):
                        pg = t % 2
                        xsem = "dsa" if pg == 0 else "dsb"
                        xt = xt2[:, pg * D:(pg + 1) * D]
                        p.add("sync", lambda e, t=t, xt=xt: e.dma_start(out=xt, in_=xo_t[t]),
                              waits=xgate[pg], inc=xsem, inc_n=16)
                        lx = W(xsem)
                        rdy, _ = ln_chain(xt, st8, 0, [(xsem, lx)])
                        p.add("dve", lambda e, xt=xt: e.tensor_scalar(
                            xnt[:, :], xt, st8[:, 2:3], st8[:, 4:5], ALU.add, ALU.mult),
                            waits=[("act", rdy), ("te", trv_prev)], inc="dve")
                        xnv = W("dve")
                        xgate[pg] = [("dve", xnv), ("act", W("act"))]
                        p.add("dve", lambda e: e.tensor_copy(xnh[:, :], xnt[:, :]),
                              waits=[("dve", xnv), ("te", lgv_prev)], inc="dve")
                        hv = W("dve")
                        p.add("dve", lambda e: e.scalar_tensor_tensor(
                            xnl[:, :], xnh[:, :], -1.0, xnt[:, :], ALU.mult, ALU.add),
                            waits=[("dve", hv)], inc="dve")
                        lv2 = W("dve")
                        for dc in range(DC):
                            p.add("te", lambda e, dc=dc: e.transpose(
                                ps_t[:, dc * 128:(dc + 1) * 128],
                                xnh[:, dc * 128:(dc + 1) * 128], eyeb[:, :]),
                                waits=[("dve", hv), ("act", W("act")),
                                       ("dve", W("dve"))], inc="te")
                        for dc in range(DC):
                            p.add("te", lambda e, dc=dc: e.transpose(
                                ps_tl[:, dc * 128:(dc + 1) * 128],
                                xnl[:, dc * 128:(dc + 1) * 128], eyeb[:, :]),
                                waits=[("dve", lv2)], inc="te")
                        trv = W("te")
                        trv_prev = trv
                        p.add("act", lambda e, t=t: e.activation(
                            xnT[:, :].rearrange("p (c n) -> p c n", c=DC)
                            [:, :, t * 128:(t + 1) * 128],
                            ps_t[:, :].rearrange("p (c n) -> p c n", c=DC),
                            AF.Copy), waits=[("te", trv)], inc="act")
                        p.add("dve", lambda e: e.tensor_copy(xnTl[:, :], ps_tl[:, :]),
                              waits=[("te", trv)], inc="dve")
                        xfv = W("dve")
                        xTh = xnT[:, :].rearrange("p (c n) -> p c n", c=DC)
                        afv1 = W("act")
                        for dc in range(DC):
                            p.add("te", lambda e, dc=dc, t=t: e.matmul(
                                ps_lg[:, 0:E],
                                xTh[:, dc, t * 128:(t + 1) * 128],
                                rwch[:, dc * E:(dc + 1) * E],
                                start=(dc == 0), stop=False),
                                waits=[("dve", xfv), ("act", afv1)], inc="te")
                            p.add("te", lambda e, dc=dc, t=t: e.matmul(
                                ps_lg[:, 0:E],
                                xTh[:, dc, t * 128:(t + 1) * 128],
                                rwcl[:, dc * E:(dc + 1) * E],
                                start=False, stop=False), inc="te")
                            p.add("te", lambda e, dc=dc: e.matmul(
                                ps_lg[:, 0:E],
                                xnTl[:, dc * 128:(dc + 1) * 128],
                                rwch[:, dc * E:(dc + 1) * E],
                                start=False, stop=(dc == DC - 1)), inc="te")
                        lgv = W("te")
                        lgv_prev = lgv
                        cw_t = cwb[:, t * 16:t * 16 + E]
                        p.add("dve", lambda e: e.tensor_tensor(
                            lgs[:, 0:E], ps_lg[:, 0:E], rbbc[:, :], ALU.add),
                            waits=[("te", lgv)], inc="dve")
                        av = W("dve")
                        p.add("dve", lambda e: e.reduce_max(
                            lgs3[:, 0:1], lgs[:, 0:E], axis=AX.X, negate=True),
                            waits=[("dve", av)], inc="dve")
                        mv = W("dve")
                        p.add("act", lambda e: e.activation(
                            lgs2[:, 0:E], lgs[:, 0:E], AF.Exp, bias=lgs3[:, 0:1]),
                            waits=[("dve", mv)], inc="act")
                        ev = W("act")
                        p.add("dve", lambda e: e.reduce_sum(
                            lgs3[:, 1:2], lgs2[:, 0:E], axis=AX.X),
                            waits=[("act", ev)], inc="dve")
                        sv = W("dve")
                        p.add("dve", lambda e: e.reciprocal(lgs3[:, 2:3], lgs3[:, 1:2]),
                              waits=[("dve", sv)], inc="dve")
                        rv = W("dve")
                        p.add("dve", lambda e: e.tensor_scalar(
                            lgs2[:, 0:E], lgs2[:, 0:E], lgs3[:, 2:3], None, ALU.mult),
                            waits=[("dve", rv)], inc="dve")
                        pv = W("dve")
                        p.add("dve", lambda e: e.tensor_copy(lgs[:, 0:E], lgs2[:, 0:E]),
                              waits=[("dve", pv)], inc="dve")
                        wv = W("dve")
                        for k in range(3):
                            p.add("dve", lambda e: e.reduce_max(
                                lgs3[:, 3:4], lgs[:, 0:E], axis=AX.X),
                                waits=[("dve", wv)], inc="dve")
                            kvv = W("dve")
                            p.add("dve", lambda e: e.tensor_scalar(
                                msk[:, 0:E], lgs[:, 0:E], lgs3[:, 3:4], None,
                                ALU.is_equal), waits=[("dve", kvv)], inc="dve")
                            mk = W("dve")
                            if k == 0:
                                p.add("dve", lambda e, cw_t=cw_t: e.tensor_tensor(
                                    cw_t, msk[:, 0:E], lgs2[:, 0:E], ALU.mult),
                                    waits=[("dve", mk)], inc="dve")
                            else:
                                p.add("dve", lambda e: e.tensor_tensor(
                                    msk[:, 16:16 + E], msk[:, 0:E], lgs2[:, 0:E],
                                    ALU.mult), waits=[("dve", mk)], inc="dve")
                                tv2 = W("dve")
                                p.add("dve", lambda e, cw_t=cw_t: e.tensor_tensor(
                                    cw_t, cw_t, msk[:, 16:16 + E], ALU.add),
                                    waits=[("dve", tv2)], inc="dve")
                            cwv = W("dve")
                            if k < 2:
                                p.add("dve", lambda e: e.scalar_tensor_tensor(
                                    lgs[:, 0:E], msk[:, 0:E], -1e30, lgs[:, 0:E],
                                    ALU.mult, ALU.add), waits=[("dve", cwv)], inc="dve")
                                wv = W("dve")
                    cw_done = W("dve")

                with ExitStack() as s2s:
                    T2 = s2s.enter_context
                    x4 = [T2(nc.sbuf_tensor(f"x4_{i}", [128, 4 * D], F32))
                          for i in range(2)]
                    xn4 = T2(nc.sbuf_tensor("xn4", [128, 4 * D], F32))
                    st2x = T2(nc.sbuf_tensor("st2x", [128, 24], F32))
                    xn2b = T2(nc.sbuf_tensor("xn2b", [128, D], BF16))
                    x2Tb = T2(nc.sbuf_tensor("x2Tb", [128, DC * 512], BF16))
                    kvsc = T2(nc.sbuf_tensor("kvsc", [128, 512], BF16))
                    qksq = T2(nc.sbuf_tensor("qksq", [128, 512], F32))
                    qksqh = T2(nc.sbuf_tensor("qksqh", [128, 512], BF16))
                    qksql = T2(nc.sbuf_tensor("qksql", [128, 512], BF16))
                    ss2 = T2(nc.sbuf_tensor("ss2", [2, 512], F32))
                    ss1 = T2(nc.sbuf_tensor("ss1", [1, 512], F32))
                    ssh = T2(nc.sbuf_tensor("ssh", [2, 2 * 512], BF16))
                    ssl = T2(nc.sbuf_tensor("ssl", [2, 2 * 512], BF16))
                    rbq = T2(nc.sbuf_tensor("rbq", [128, 512], F32))
                    rbk = T2(nc.sbuf_tensor("rbk", [64, 512], F32))
                    ps_tb = T2(nc.psum_tensor("ps_tb", [128, D], BF16))
                    ps_q = T2(nc.psum_tensor("ps_q", [128, 512], F32))
                    ps_kv = T2(nc.psum_tensor("ps_kv", [128, 512], F32))
                    ps_s2 = T2(nc.psum_tensor("ps_s2", [2, 512], F32))
                    ps_s1 = T2(nc.psum_tensor("ps_s1", [1, 512], F32))
                    ps_bc = T2(nc.psum_tensor("ps_bc", [128, 512], F32))
                    ps_vt = T2(nc.psum_tensor("ps_vt", [128, 1024], BF16))

                    xf_t = x_full.rearrange("(t p) d -> t p d", p=128)
                    p.phase = "p2"
                    p.barrier()
                    g0 = [("dve", cw_done), ("act", W("act"))]
                    xgate = {0: list(g0), 1: list(g0)}
                    dve_blk_prev = 0
                    projte_prev = 0
                    xn4gate = 0
                    vt_prev = 0
                    qbm_prev = 0
                    kbm_prev = 0
                    ksm_prev = 0
                    qsm_prev = 0
                    x2v_gate = 0
                    for blk in range(N // 512 if "p2" in PH else 0):
                        pg = blk % 2
                        xsem = "dsa" if pg == 0 else "dsb"
                        xb = x4[pg]
                        lxs = []
                        for tt in range(4):
                            t = blk * 4 + tt
                            p.add("sync", lambda e, t=t, tt=tt, xb=xb: e.dma_start(
                                out=xb[:, tt * D:(tt + 1) * D], in_=xf_t[t]),
                                waits=xgate[pg] if tt == 0 else [],
                                inc=xsem, inc_n=16)
                            lxs.append(W(xsem))
                        for tt in range(4):
                            p.add("dve", lambda e, tt=tt, xb=xb: e.reduce_sum(
                                st2x[:, tt:tt + 1], xb[:, tt * D:(tt + 1) * D],
                                axis=AX.X), waits=[(xsem, lxs[tt])], inc="dve")
                        rds = W("dve")
                        for tt in range(4):
                            p.add("act", lambda e, tt=tt, xb=xb: e.activation(
                                sqscr[:, :], xb[:, tt * D:(tt + 1) * D], AF.Square,
                                accum_out=st2x[:, 4 + tt:5 + tt]),
                                waits=[(xsem, lxs[tt])], inc="act")
                        sqs = W("act")
                        p.add("dve", lambda e: e.tensor_scalar(
                            st2x[:, 8:12], st2x[:, 0:4], -1.0 / D, None, ALU.mult),
                            waits=[("dve", rds)], inc="dve")
                        v1 = W("dve")
                        p.add("dve", lambda e: e.tensor_tensor(
                            st2x[:, 0:4], st2x[:, 8:12], st2x[:, 8:12], ALU.mult),
                            waits=[("dve", v1)], inc="dve")
                        v2 = W("dve")
                        p.add("dve", lambda e: e.tensor_scalar(
                            st2x[:, 12:16], st2x[:, 4:8], 1.0 / D, None, ALU.mult),
                            waits=[("act", sqs), ("dve", v2)], inc="dve")
                        v3 = W("dve")
                        p.add("dve", lambda e: e.tensor_tensor(
                            st2x[:, 12:16], st2x[:, 12:16], st2x[:, 0:4],
                            ALU.subtract), waits=[("dve", v3)], inc="dve")
                        v4 = W("dve")
                        p.add("act", lambda e: e.activation(
                            st2x[:, 16:20], st2x[:, 12:16], AF.Ln, bias=ceps[:, 0:1]),
                            waits=[("dve", v4)], inc="act")
                        a1 = W("act")
                        p.add("act", lambda e: e.activation(
                            st2x[:, 16:20], st2x[:, 16:20], AF.Exp,
                            scale=cnhalf[:, 0:1]), waits=[("act", a1)], inc="act")
                        a2 = W("act")
                        p.add("dve", lambda e: e.tensor_tensor(
                            st2x[:, 8:12], st2x[:, 8:12], st2x[:, 16:20], ALU.mult),
                            waits=[("act", a2)], inc="dve")
                        nmrs = W("dve")
                        for tt in range(4):
                            p.add("act", lambda e, tt=tt, xb=xb: e.activation(
                                xn4[:, tt * D:(tt + 1) * D],
                                xb[:, tt * D:(tt + 1) * D], AF.Identity,
                                bias=st2x[:, 8 + tt:9 + tt],
                                scale=st2x[:, 16 + tt:17 + tt]),
                                waits=[("dve", nmrs), ("act", xn4gate)], inc="act")
                        ap1 = W("act")
                        xgate[pg] = [("act", ap1)]
                        for tt in range(4):
                            p.add("dve", lambda e, tt=tt: e.reduce_sum(
                                st2x[:, tt:tt + 1], xn4[:, tt * D:(tt + 1) * D],
                                axis=AX.X), waits=[("act", ap1)], inc="dve")
                        rds2 = W("dve")
                        for tt in range(4):
                            p.add("act", lambda e, tt=tt: e.activation(
                                sqscr[:, :], xn4[:, tt * D:(tt + 1) * D], AF.Square,
                                accum_out=st2x[:, 4 + tt:5 + tt]),
                                waits=[("act", ap1)], inc="act")
                        sqs2 = W("act")
                        p.add("dve", lambda e: e.tensor_scalar(
                            st2x[:, 8:12], st2x[:, 0:4], -1.0 / D, None, ALU.mult),
                            waits=[("dve", rds2)], inc="dve")
                        v1 = W("dve")
                        p.add("dve", lambda e: e.tensor_tensor(
                            st2x[:, 0:4], st2x[:, 8:12], st2x[:, 8:12], ALU.mult),
                            waits=[("dve", v1)], inc="dve")
                        v2 = W("dve")
                        p.add("dve", lambda e: e.tensor_scalar(
                            st2x[:, 12:16], st2x[:, 4:8], 1.0 / D, None, ALU.mult),
                            waits=[("act", sqs2), ("dve", v2)], inc="dve")
                        v3 = W("dve")
                        p.add("dve", lambda e: e.tensor_tensor(
                            st2x[:, 12:16], st2x[:, 12:16], st2x[:, 0:4],
                            ALU.subtract), waits=[("dve", v3)], inc="dve")
                        v4 = W("dve")
                        p.add("act", lambda e: e.activation(
                            st2x[:, 16:20], st2x[:, 12:16], AF.Ln, bias=ceps[:, 0:1]),
                            waits=[("dve", v4)], inc="act")
                        a1 = W("act")
                        p.add("act", lambda e: e.activation(
                            st2x[:, 16:20], st2x[:, 16:20], AF.Exp,
                            scale=cnhalf[:, 0:1]), waits=[("act", a1)], inc="act")
                        a2 = W("act")
                        p.add("dve", lambda e: e.tensor_tensor(
                            st2x[:, 8:12], st2x[:, 8:12], st2x[:, 16:20], ALU.mult),
                            waits=[("act", a2)], inc="dve")
                        nmrs2 = W("dve")
                        for tt in range(4):
                            p.add("act", lambda e, tt=tt: e.activation(
                                xn2b[:, :], xn4[:, tt * D:(tt + 1) * D], AF.Identity,
                                bias=st2x[:, 8 + tt:9 + tt],
                                scale=st2x[:, 16 + tt:17 + tt]),
                                waits=[("dve", nmrs2), ("te", x2v_gate)], inc="act")
                            x2v = W("act")
                            for dc in range(DC):
                                p.add("te", lambda e, dc=dc: e.transpose(
                                    ps_tb[:, dc * 128:(dc + 1) * 128],
                                    xn2b[:, dc * 128:(dc + 1) * 128], eyeb[:, :]),
                                    waits=[("act", x2v)], inc="te")
                            trv = W("te")
                            x2v_gate = trv
                            p.add("act", lambda e, tt=tt: e.activation(
                                x2Tb[:, :].rearrange("p (c n) -> p c n", c=DC)
                                [:, :, tt * 128:(tt + 1) * 128],
                                ps_tb[:, :].rearrange("p (c n) -> p c n", c=DC),
                                AF.Copy),
                                waits=[("te", trv), ("te", projte_prev)], inc="act")
                        xn4gate = W("act")
                        blkv = W("act")

                        for dc in range(DC):
                            p.add("te", lambda e, dc=dc: e.matmul(
                                ps_q[:, :], wqT[:, dc * 128:(dc + 1) * 128],
                                x2Tb[:, dc * 512:(dc + 1) * 512],
                                start=(dc == 0), stop=(dc == DC - 1)),
                                waits=([("act", blkv), ("dve", dve_blk_prev)]
                                       if dc == 0 else []), inc="te")
                        qv = W("te")
                        for dc in range(DC):
                            p.add("te", lambda e, dc=dc: e.matmul(
                                ps_kv[:, :], wkvT[:, dc * 128:(dc + 1) * 128],
                                x2Tb[:, dc * 512:(dc + 1) * 512],
                                start=(dc == 0), stop=(dc == DC - 1)), inc="te")
                        kvv = W("te")
                        projte_prev = kvv
                        p.add("act", lambda e: e.activation(
                            kvsc[:, :], ps_kv[:, :], AF.Copy),
                            waits=[("te", kvv), ("te", vt_prev)], inc="act")
                        kvev = W("act")
                        p.add("act", lambda e: e.activation(
                            qksq[:, :], ps_q[:, :], AF.Square),
                            waits=[("te", qv), ("te", qbm_prev)], inc="act")
                        qsv = W("act")
                        p.add("dve", lambda e: e.tensor_copy(qksqh[:, :], qksq[:, :]),
                              waits=[("act", qsv), ("te", qsm_prev)], inc="dve")
                        qh1 = W("dve")
                        p.add("dve", lambda e: e.scalar_tensor_tensor(
                            qksql[:, :], qksqh[:, :], -1.0, qksq[:, :],
                            ALU.mult, ALU.add), waits=[("dve", qh1)], inc="dve")
                        qh2 = W("dve")
                        p.add("te", lambda e: e.matmul(
                            ps_s2[:, :], qssT[:, :], qksqh[:, :], start=True, stop=False),
                            waits=[("dve", qh2)], inc="te")
                        p.add("te", lambda e: e.matmul(
                            ps_s2[:, :], qssT[:, :], qksql[:, :], start=False, stop=True),
                            inc="te")
                        qsm = W("te")
                        p.add("act", lambda e: e.activation(
                            qksq[0:64, :], kvsc[0:64, :], AF.Square),
                            waits=[("act", kvev), ("te", qsm), ("dve", qh2)],
                            inc="act")
                        ksv = W("act")
                        p.add("dve", lambda e: e.tensor_copy(
                            qksqh[0:64, :], qksq[0:64, :]),
                            waits=[("act", ksv), ("te", qsm)], inc="dve")
                        kh1 = W("dve")
                        p.add("dve", lambda e: e.scalar_tensor_tensor(
                            qksql[0:64, :], qksqh[0:64, :], -1.0, qksq[0:64, :],
                            ALU.mult, ALU.add), waits=[("dve", kh1)], inc="dve")
                        kh2 = W("dve")
                        p.add("te", lambda e: e.matmul(
                            ps_s1[:, :], qssT[0:64, 0:1], qksqh[0:64, :],
                            start=True, stop=False), waits=[("dve", kh2)], inc="te")
                        p.add("te", lambda e: e.matmul(
                            ps_s1[:, :], qssT[0:64, 0:1], qksql[0:64, :],
                            start=False, stop=True), inc="te")
                        ksm = W("te")
                        ksm_prev = ksm
                        qsm_prev = qsm
                        p.add("dve", lambda e: e.tensor_scalar(
                            ss2[:, :], ps_s2[:, :], 1.0, 3.90625e-27,
                            ALU.mult, ALU.max),
                            waits=[("te", qsm), ("te", qbm_prev)], inc="dve")
                        qnv = W("dve")
                        p.add("act", lambda e: e.activation(ss2[:, :], ss2[:, :], AF.Ln),
                              waits=[("dve", qnv)], inc="act")
                        p.add("act", lambda e: e.activation(
                            ss2[:, :], ss2[:, :], AF.Exp, bias=cln4[0:2, 0:1],
                            scale=cnhalf[0:2, 0:1]),
                            waits=[("act", W("act"))], inc="act")
                        qfv = W("act")
                        p.add("dve", lambda e: e.tensor_scalar(
                            ss1[:, :], ps_s1[:, :], 1.0, 1e-24, ALU.mult, ALU.max),
                            waits=[("te", ksm), ("te", kbm_prev)], inc="dve")
                        knv = W("dve")
                        p.add("act", lambda e: e.activation(ss1[:, :], ss1[:, :], AF.Ln),
                              waits=[("dve", knv)], inc="act")
                        p.add("act", lambda e: e.activation(
                            ss1[:, :], ss1[:, :], AF.Exp, bias=cln4[0:1, 0:1],
                            scale=cnhalf[0:1, 0:1]),
                            waits=[("act", W("act"))], inc="act")
                        kfv = W("act")
                        p.add("dve", lambda e: e.tensor_copy(ssh[:, 0:512], ss2[:, :]),
                              waits=[("act", qfv), ("te", qbm_prev)], inc="dve")
                        qb1 = W("dve")
                        p.add("dve", lambda e: e.scalar_tensor_tensor(
                            ssl[:, 0:512], ssh[:, 0:512], -1.0, ss2[:, :],
                            ALU.mult, ALU.add), waits=[("dve", qb1)], inc="dve")
                        qb2 = W("dve")
                        p.add("te", lambda e: e.matmul(
                            ps_bc[:, :], bc2b[:, :], ssh[:, 0:512],
                            start=True, stop=False), waits=[("dve", qb2)], inc="te")
                        p.add("te", lambda e: e.matmul(
                            ps_bc[:, :], bc2b[:, :], ssl[:, 0:512],
                            start=False, stop=True), inc="te")
                        qbm = W("te")
                        qbm_prev = qbm
                        p.add("dve", lambda e: e.tensor_copy(rbq[:, :], ps_bc[:, :]),
                              waits=[("te", qbm)], inc="dve")
                        qbe = W("dve")
                        p.add("dve", lambda e, blk=blk: e.scalar_tensor_tensor(
                            qhT[:, blk * 512:(blk + 1) * 512], ps_q[:, :], 1.0,
                            rbq[:, :], ALU.mult, ALU.mult),
                            waits=[("dve", qbe)], inc="dve")
                        qhv = W("dve")
                        p.add("dve", lambda e: e.tensor_copy(ssh[0:1, 512:1024], ss1[:, :]),
                              waits=[("act", kfv), ("te", kbm_prev)], inc="dve")
                        kb1 = W("dve")
                        p.add("dve", lambda e: e.scalar_tensor_tensor(
                            ssl[0:1, 512:1024], ssh[0:1, 512:1024], -1.0, ss1[:, :],
                            ALU.mult, ALU.add), waits=[("dve", kb1)], inc="dve")
                        kb2 = W("dve")
                        p.add("te", lambda e: e.matmul(
                            ps_bc[0:64, :], bc2b[0:1, 0:64], ssh[0:1, 512:1024],
                            start=True, stop=False),
                            waits=[("dve", kb2), ("dve", qhv)], inc="te")
                        p.add("te", lambda e: e.matmul(
                            ps_bc[0:64, :], bc2b[0:1, 0:64], ssl[0:1, 512:1024],
                            start=False, stop=True), inc="te")
                        kbm = W("te")
                        kbm_prev = kbm
                        p.add("dve", lambda e: e.tensor_copy(rbk[:, :], ps_bc[0:64, :]),
                              waits=[("te", kbm)], inc="dve")
                        kbe = W("dve")
                        p.add("dve", lambda e, blk=blk: e.scalar_tensor_tensor(
                            kbuf[0:64, blk * 512:(blk + 1) * 512], ps_kv[0:64, :],
                            1.0, rbk[:, :], ALU.mult, ALU.mult),
                            waits=[("dve", kbe)], inc="dve")
                        khv = W("dve")
                        for cc4 in range(4):
                            g = blk * 4 + cc4
                            p.add("te", lambda e, cc4=cc4: e.transpose(
                                ps_vt[:, 0:DH], kvsc[64:128, cc4 * 128:(cc4 + 1) * 128],
                                eyeb[64:128, 64:128]),
                                waits=[("act", kvev), ("dve", W("dve"))], inc="te")
                            vtv = W("te")
                            p.add("dve", lambda e, g=g: e.tensor_copy(
                                vbuf[:, g * (DH + 1): g * (DH + 1) + DH], ps_vt[:, 0:DH]),
                                waits=[("te", vtv)], inc="dve")
                        vt_prev = W("te")
                        dve_blk_prev = W("dve")

                    if "vones" not in KSKIP:
                        p.add("dve", lambda e: e.memset(
                            vbuf[:, :].rearrange("p (c w) -> p c w", w=DH + 1)
                            [:, :, DH:DH + 1], 1.0), inc="dve")
                    prep_dve = W("dve")
                    if "kdup" not in KSKIP:
                        p.add("gps", lambda e: e.dma_start(
                            out=kbuf[64:128, :], in_=kbuf[0:64, :]),
                            waits=[("dve", prep_dve)], inc="dgp", inc_n=16)
                    kdup = W("dgp")

            p.phase = "p3"
            p.barrier()
            with ExitStack() as sc_at:
                Ta = sc_at.enter_context
                ptb = Ta(nc.sbuf_tensor("ptb", [128, 4 * 1024], BF16))
                rsum = Ta(nc.sbuf_tensor("rsum", [128, 512], F32))
                rshl = Ta(nc.sbuf_tensor("rshl", [128, 2 * 512], BF16))
                atile = Ta(nc.sbuf_tensor("atile", [64, 4 * 512], BF16))
                rbs = Ta(nc.sbuf_tensor("rbs", [64, 512], F32))
                ps_S = [Ta(nc.psum_tensor(f"ps_S{i}", [128, 1024], F32))
                        for i in range(B)]
                ps_O = [[Ta(nc.psum_tensor(f"ps_O{i}_{h}", [65, 512], F32))
                         for h in range(HPC)] for i in range(B)]

                mm1vals = [{} for _ in range(B)]
                expsem = [{} for _ in range(B)]
                mm2te = [{} for _ in range(B)]
                odiv = {(b, h): 0 for b in range(B) for h in range(HPC)}
                sdiv = {b: 0 for b in range(B)}
                aslot_gate = {i: ("dg%d" % i, 0) for i in range(4)}
                rbv_prev = 0
                rbe_prev = 0
                aidx = 0

                def emit_mm1(b, j, c):
                    klo = b * S + c * KC
                    qlo = b * S + j * QB
                    w = []
                    if c == 0:
                        w = [("dve", sdiv[b])]
                        if j == 0 and b == 0:
                            w += [("dgp", kdup), ("dve", prep_dve)]
                    elif c >= 1:
                        w = [expsem[b][c - 1]]
                    for hh in range(HPC):
                        p.add("te", lambda e, hh=hh, b=b, klo=klo, qlo=qlo: e.matmul(
                            ps_S[b][:, hh * 512:(hh + 1) * 512],
                            kbuf[hh * 64:hh * 64 + 64, klo:klo + KC],
                            qhT[hh * 64:hh * 64 + 64, qlo:qlo + QB],
                            start=True, stop=True),
                            waits=w if hh == 0 else [], inc="te")
                    mm1vals[b][c] = W("te")

                def emit_exp(b, j, c):
                    pp = b * 2 + (c % 2)
                    w = [("te", mm1vals[b][c])]
                    if c >= 2:
                        w.append(("te", mm2te[b][c - 2]))
                    p.add("act", lambda e, b=b, pp=pp: e.activation(
                        ptb[:, pp * 1024:(pp + 1) * 1024], ps_S[b][:, :], AF.Exp),
                        waits=w, inc="act")
                    r = c - RPQ * j
                    if 0 <= r < RPQ:
                        ev = W("act")
                        for hh in range(HPC):
                            p.add("dve", lambda e, pp=pp, r=r, hh=hh: e.tensor_tensor(
                                ptb[:, pp * 1024 + hh * 512: pp * 1024 + (hh + 1) * 512],
                                ptb[:, pp * 1024 + hh * 512: pp * 1024 + (hh + 1) * 512],
                                maskb[:, r * QB:(r + 1) * QB], ALU.mult),
                                waits=[("act", ev)] if hh == 0 else [], inc="dve")
                        expsem[b][c] = ("dve", W("dve"))
                    else:
                        expsem[b][c] = ("act", W("act"))

                def emit_mm2(b, j, c, nch):
                    pp = b * 2 + (c % 2)
                    g = (b * S) // KC + c
                    es, evv = expsem[b][c]
                    for hh in range(HPC):
                        w = [(es, evv)]
                        if c == 0:
                            w.append(("dve", odiv[(b, hh)]))
                        p.add("te", lambda e, hh=hh, b=b, pp=pp, g=g, c=c, nch=nch: e.matmul(
                            ps_O[b][hh][:, :],
                            vbuf[:, g * (DH + 1):(g + 1) * (DH + 1)],
                            ptb[:, pp * 1024 + hh * 512: pp * 1024 + (hh + 1) * 512],
                            start=(c == 0), stop=(c == nch - 1)),
                            waits=w if (hh == 0 or c == 0) else [], inc="te")
                    mm2te[b][c] = W("te")

                def emit_div(b, j, nch):
                    nonlocal rbv_prev, rbe_prev, aidx
                    qlo = b * S + j * QB
                    for hh in range(HPC):
                        p.add("dve", lambda e, b=b, hh=hh: e.reciprocal(
                            rsum[64:65, :], ps_O[b][hh][64:65, :]),
                            waits=[("te", mm2te[b][nch - 1]), ("te", rbv_prev)],
                            inc="dve")
                        rcv = W("dve")
                        p.add("dve", lambda e: e.tensor_copy(
                            rshl[64:65, 0:512], rsum[64:65, :]),
                            waits=[("dve", rcv)], inc="dve")
                        rh1 = W("dve")
                        p.add("dve", lambda e: e.scalar_tensor_tensor(
                            rshl[64:65, 512:1024], rshl[64:65, 0:512], -1.0,
                            rsum[64:65, :], ALU.mult, ALU.add),
                            waits=[("dve", rh1)], inc="dve")
                        rh2 = W("dve")
                        p.add("te", lambda e, b=b: e.matmul(
                            ps_S[b][0:64, 0:512], onesb[64:65, 0:64],
                            rshl[64:65, 0:512], start=True, stop=False),
                            waits=[("dve", rh2), ("dve", rbe_prev),
                                   expsem[b][nch - 1]], inc="te")
                        p.add("te", lambda e, b=b: e.matmul(
                            ps_S[b][0:64, 0:512], onesb[64:65, 0:64],
                            rshl[64:65, 512:1024], start=False, stop=True), inc="te")
                        rbv = W("te")
                        rbv_prev = rbv
                        p.add("dve", lambda e, b=b: e.tensor_copy(
                            rbs[:, :], ps_S[b][0:64, 0:512]),
                            waits=[("te", rbv)], inc="dve")
                        rbe = W("dve")
                        rbe_prev = rbe
                        slot = aidx % 4
                        p.add("dve", lambda e, b=b, hh=hh, slot=slot: e.scalar_tensor_tensor(
                            atile[:, slot * 512:(slot + 1) * 512],
                            ps_O[b][hh][0:64, :], 1.0, rbs[:, :], ALU.mult, ALU.mult),
                            waits=[("dve", rbe), aslot_gate[slot]], inc="dve")
                        mlv = W("dve")
                        odiv[(b, hh)] = mlv
                        asem = f"dg{slot}"
                        p.add("gps", lambda e, hh=hh, slot=slot, qlo=qlo: e.dma_start(
                            out=cc_in.ap()[hh * 64:(hh + 1) * 64, qlo:qlo + QB],
                            in_=atile[:, slot * 512:(slot + 1) * 512]),
                            waits=[("dve", mlv)], inc=asem, inc_n=16)
                        aslot_gate[slot] = (asem, W(asem))
                        aidx += 1
                    sdiv[b] = rbe_prev

                for j in range(NQB if "p3" in PH else 0):
                    nch = RPQ * (j + 1)
                    for c in range(nch):
                        for b in range(B):
                            emit_mm1(b, j, c)
                        for b in range(B):
                            emit_exp(b, j, c)
                        if c > 0:
                            for b in range(B):
                                emit_mm2(b, j, c - 1, nch)
                    for b in range(B):
                        emit_mm2(b, j, nch - 1, nch)
                    for b in range(B):
                        emit_div(b, j, nch)

                if "cc" in PH:
                    p.add("gps", lambda e: e.collective_compute(
                        "AllGather", ALU.bypass, replica_groups=[list(range(NCORES))],
                        ins=[cc_in.ap().opt()], outs=[cc_out.ap().opt()]),
                        waits=[("dgp", W("dgp")), ("dg0", W("dg0")),
                               ("dg1", W("dg1")), ("dg2", W("dg2")),
                               ("dg3", W("dg3"))], inc="cc", inc_n=None)
                if dbg:
                    bw = [("te", W("te")), ("act", W("act")), ("dve", W("dve"))]
                    p.add("gps", lambda e: e.dma_start(out=dbg_k, in_=kbuf[:, :]),
                          waits=bw, inc="dgp", inc_n=16)
                    p.add("gps", lambda e: e.dma_start(out=dbg_q, in_=qhT[:, :]),
                          inc="dgp", inc_n=16)
                    p.add("gps", lambda e: e.dma_start(out=dbg_v, in_=vbuf[:, :]),
                          inc="dgp", inc_n=16)
                    p.add("gps", lambda e: e.dma_start(out=dbg_xnT, in_=xnT[:, :]),
                          inc="dgp", inc_n=16)
                    p.add("gps", lambda e: e.dma_start(out=dbg_cw, in_=cwb[:, :]),
                          inc="dgp", inc_n=16)
                    p.add("gps", lambda e: e.dma_start(out=dbg_att, in_=cc_in.ap()),
                          waits=[("cc", 1)], inc="dgp", inc_n=16)
                    dbg_gate = W("dgp")

        p.phase = "p5"
        p.barrier()
        with ExitStack() as sc_f:
            Tf = sc_f.enter_context
            gacc = Tf(nc.sbuf_tensor("gacc", [128, DC * TOWN], F32))

            if "p5" not in PH:
                p.add("dve", lambda e: e.memset(gacc[:, :], 0.5), inc="dve")
            with ExitStack() as sc5:
                T5 = sc5.enter_context
                cwT = T5(nc.sbuf_tensor("cwT", [16, TOWN], BF16))
                cwbh = T5(nc.sbuf_tensor("cwbh", [128, 16], BF16))
                ohot = T5(nc.sbuf_tensor("ohot", [E, E * 128], BF16))
                p.add("sync", lambda e: e.dma_start(out=ohot[:, :], in_=ohot_i),
                      inc="dsy", inc_n=16)
                ohot_ld = W("dsy")
                hbuf = T5(nc.sbuf_tensor("hbuf", [128, HC * HTOK], BF16))
                wring = T5(nc.sbuf_tensor("wring", [128, 2 * 512], F32))
                w1b = [T5(nc.sbuf_tensor(f"w1b{i}", [128, DC * HF], BF16))
                       for i in range(2)]
                w2b = [T5(nc.sbuf_tensor(f"w2b{i}", [128, HC * D], BF16))
                       for i in range(2)]
                cwbc = T5(nc.sbuf_tensor("cwbc", [128, HTOK], F32))
                yscr = T5(nc.sbuf_tensor("yscr", [128, TB], F32))
                ps_cw = T5(nc.psum_tensor("ps_cw", [16, 1024], BF16))
                ps_H = [T5(nc.psum_tensor(f"ps_H{i}", [128, 512], F32))
                        for i in range(TGRP)]
                ps_Y = [T5(nc.psum_tensor(f"ps_Y{i}", [128, 512], F32))
                        for i in range(2)]
                ps_cb = [T5(nc.psum_tensor(f"ps_cb{i}", [128, 512], F32))
                         for i in range(2)]

                p.phase = "p5x"
                cwt_e_prev = 0
                cwt_te_prev = 0
                for t in range(OTILE if "p5" in PH else 0):
                    p.add("dve", lambda e, t=t: e.tensor_copy(
                        cwbh[:, :], cwb[:, t * 16:(t + 1) * 16]),
                        waits=[("dve", cw_done), ("te", cwt_te_prev)], inc="dve")
                    cv5 = W("dve")
                    p.add("te", lambda e, t=t: e.transpose(
                        ps_cw[:, 0:128], cwbh[:, :], eyeb[:, :]),
                        waits=[("dve", cv5), ("dve", cwt_e_prev)], inc="te")
                    tv = W("te")
                    cwt_te_prev = tv
                    p.add("dve", lambda e, t=t: e.tensor_copy(
                        cwT[0:E, t * 128:(t + 1) * 128], ps_cw[0:E, 0:128]),
                        waits=[("te", tv)], inc="dve")
                    cwt_e_prev = W("dve")
                cwtv = W("dve")

                p.phase = "p5"
                mats = [(sw1_i, "w1"), (sw2_i, "w2")]
                for ee in range(E):
                    mats.append((ew1_i[ee], "w1"))
                    mats.append((ew2_i[ee], "w2"))
                conv_done = {}
                ringlast = [0, 0]
                slot_eng = ["sync", "act"]
                slot_sem = ["wr0", "wr1"]
                wch = [0]
                feedq = []

                def queue_matrix(m, gate_te):
                    src, kindm = mats[m]
                    buf = (w1b if kindm == "w1" else w2b)[(m // 2) % 2]
                    for ch in range(2 * DC):
                        feedq.append((m, ch, src, buf, gate_te))

                def feed(n=1):
                    for _ in range(n):
                        if not feedq:
                            return
                        m, ch, src, buf, gate_te = feedq.pop(0)
                        ph_save = p.phase
                        p.phase = "p5w"
                        slot = wch[0] % 2
                        wch[0] += 1
                        wsem = slot_sem[slot]
                        r0 = (ch // 2) * 128
                        c0 = (ch % 2) * 512
                        gate = ([("dve", ringlast[slot])] if ringlast[slot] else [])
                        p.add(slot_eng[slot], lambda e, src=src, r0=r0, c0=c0, slot=slot: e.dma_start(
                            out=wring[:, slot * 512:(slot + 1) * 512],
                            in_=src[r0:r0 + 128, c0:c0 + 512]),
                            waits=gate, inc=wsem, inc_n=16)
                        lv = W(wsem)
                        p.add("dve", lambda e, r0=r0, c0=c0, slot=slot, buf=buf: e.tensor_copy(
                            buf[:, r0 * 8 + c0:r0 * 8 + c0 + 512],
                            wring[:, slot * 512:(slot + 1) * 512]),
                            waits=[(wsem, lv), ("te", gate_te)], inc="dve")
                        ringlast[slot] = W("dve")
                        conv_done[m] = W("dve")
                        p.phase = ph_save

                if "p5" in PH:
                    queue_matrix(0, 0)
                    queue_matrix(1, 0)
                    feed(len(feedq))

                cb_evac = {0: 0, 1: 0}
                hevac = {i: 0 for i in range(2 * TGRP)}
                yevac = {0: 0, 1: 0}
                yevac_a = {0: 0, 1: 0}
                gacc_last = {}
                hbuf_gate = 0

                te_end = {}
                for m in range(0, len(mats) if "p5" in PH else 0, 2):
                    ee = m // 2 - 1
                    pbuf = (m // 2) % 2
                    W1 = w1b[pbuf]
                    W2 = w2b[pbuf]
                    if m + 2 < len(mats):
                        g = W("te")
                        queue_matrix(m + 2, g)
                        queue_matrix(m + 3, g)
                    for grp in range(NGRP):
                        t0 = grp * HTOK
                        if ee >= 0:
                            for tb in range(TGRP):
                                cb = tb % 2
                                p.add("te", lambda e, ee=ee, t0=t0, tb=tb, cb=cb: e.matmul(
                                    ps_cb[cb][:, 0:TB], ohot[:, ee * 128:(ee + 1) * 128],
                                    cwT[0:E, t0 + tb * TB:t0 + (tb + 1) * TB],
                                    start=True, stop=True),
                                    waits=[("dve", cwtv), ("dsy", ohot_ld),
                                           ("dve", cb_evac[cb])], inc="te")
                                tv = W("te")
                                p.add("dve", lambda e, tb=tb, cb=cb: e.tensor_copy(
                                    cwbc[:, tb * TB:(tb + 1) * TB], ps_cb[cb][:, 0:TB]),
                                    waits=[("te", tv)], inc="dve")
                                cb_evac[cb] = W("dve")
                            cwbv = W("dve")
                        for hc in range(HC):
                            hp = 0
                            for dc in range(DC):
                                for tb in range(TGRP):
                                    w = []
                                    if dc == 0:
                                        w = [("dve", conv_done[m]),
                                             ("act", hevac[hp + tb])]
                                    p.add("te", lambda e, hc=hc, dc=dc, tb=tb,
                                          t0=t0, W1=W1, hp=hp: e.matmul(
                                        ps_H[hp + tb][:, 0:TB],
                                        W1[:, dc * HF + hc * 128:
                                           dc * HF + (hc + 1) * 128],
                                        xnT[:, dc * TOWN + t0 + tb * TB:
                                            dc * TOWN + t0 + (tb + 1) * TB],
                                        start=(dc == 0), stop=(dc == DC - 1)),
                                        waits=w, inc="te")
                            m1v = W("te")
                            for tb in range(TGRP):
                                bias = (sb1c[:, hc:hc + 1] if ee < 0
                                        else eb1c[:, ee * HC + hc:ee * HC + hc + 1])
                                p.add("act", lambda e, hc=hc, tb=tb, bias=bias,
                                      t0=t0, hp=hp: e.activation(
                                    hbuf[:, hc * HTOK + tb * TB:
                                         hc * HTOK + (tb + 1) * TB],
                                    ps_H[hp + tb][:, 0:TB], AF.Relu, bias=bias),
                                    waits=[("te", m1v), ("te", hbuf_gate)], inc="act")
                                hevac[hp + tb] = W("act")
                            feed(1)
                        hdone = W("act")
                        for dc2 in range(DC):
                            for tb in range(TGRP):
                                yb = (dc2 * TGRP + tb) % 2
                                for hc in range(HC):
                                    w = []
                                    if hc == 0:
                                        w = [("act", hdone), ("dve", yevac[yb]),
                                             ("act", yevac_a[yb]),
                                             ("dve", conv_done[m + 1])]
                                    p.add("te", lambda e, dc2=dc2, tb=tb, hc=hc,
                                          W2=W2, yb=yb, t0=t0, ee=ee: e.matmul(
                                        ps_Y[yb][:, 0:TB],
                                        W2[:, hc * D + dc2 * 128:
                                           hc * D + (dc2 + 1) * 128],
                                        hbuf[:, hc * HTOK + tb * TB:
                                             hc * HTOK + (tb + 1) * TB],
                                        start=(hc == 0),
                                        stop=(hc == HC - 1 and ee >= 0)),
                                        waits=w, inc="te")
                                if ee < 0:
                                    p.add("te", lambda e, dc2=dc2, tb=tb, t0=t0,
                                          yb=yb: e.matmul(
                                        ps_Y[yb][:, 0:TB],
                                        eb2t[:, dc2 * 128:(dc2 + 1) * 128],
                                        cwT[0:E, t0 + tb * TB: t0 + (tb + 1) * TB],
                                        start=False, stop=True),
                                        waits=[("dve", cwtv)], inc="te")
                                m2v = W("te")
                                gsl = gacc[:, dc2 * TOWN + t0 + tb * TB:
                                           dc2 * TOWN + t0 + (tb + 1) * TB]
                                gkey = (dc2, grp, tb)
                                if ee < 0:
                                    p.add("act", lambda e, gsl=gsl, dc2=dc2,
                                          yb=yb: e.activation(
                                        gsl, ps_Y[yb][:, 0:TB], AF.Identity,
                                        bias=sb2c[:, dc2:dc2 + 1]),
                                        waits=[("te", m2v)], inc="act")
                                    yevac_a[yb] = W("act")
                                    gacc_last[gkey] = ("act", W("act"))
                                    feed(1)
                                else:
                                    p.add("dve", lambda e, tb=tb, yb=yb: e.scalar_tensor_tensor(
                                        yscr[:, :], ps_Y[yb][:, 0:TB], 1.0,
                                        cwbc[:, tb * TB:(tb + 1) * TB],
                                        ALU.mult, ALU.mult),
                                        waits=[("te", m2v), ("dve", cwbv)], inc="dve")
                                    sv = W("dve")
                                    gw = [("dve", sv)]
                                    if gkey in gacc_last:
                                        gw.append(gacc_last[gkey])
                                    p.add("dve", lambda e, gsl=gsl: e.tensor_tensor(
                                        gsl, gsl, yscr[:, :], ALU.add),
                                        waits=gw, inc="dve")
                                    yevac[yb] = W("dve")
                                    gacc_last[gkey] = ("dve", W("dve"))
                                feed(1)
                        hbuf_gate = W("te")
                    feed(len(feedq))
                    te_end[m // 2] = W("te")
                ffn_act = W("act")
                ffn_dve = W("dve")

            p.phase = "p6"
            p.barrier()
            with ExitStack() as sc6:
                T6 = sc6.enter_context
                afull = T6(nc.sbuf_tensor("afull", [128, DC * TOWN], BF16))
                o2scr = T6(nc.sbuf_tensor("o2scr", [128, DC * TB], BF16))
                o2t = T6(nc.sbuf_tensor("o2t", [128, D], F32))
                st6 = T6(nc.sbuf_tensor("st6", [128, 8], F32))
                stscr6 = T6(nc.sbuf_tensor("stscr6", [128, (TOWN // 128) * 8], F32))
                if dbg:
                    p.add("dve", lambda e: e.memset(stscr6[:, :], 0.0), inc="dve")
                woutT = T6(nc.sbuf_tensor("woutT", [128, DC * D], BF16))
                w6f = T6(nc.sbuf_tensor("w6f", [128, D], F32))
                w6b = T6(nc.sbuf_tensor("w6b", [128, D], BF16))
                g2bc = T6(nc.sbuf_tensor("g2bc", [128, D], F32))
                g2row = T6(nc.sbuf_tensor("g2row", [1, D], F32))
                g2hl = T6(nc.sbuf_tensor("g2hl", [1, 2 * D], BF16))
                stile = T6(nc.sbuf_tensor("stile", [128, 2 * D], BF16))
                ps_o2 = [T6(nc.psum_tensor(f"ps_o2{i}", [128, 512], F32))
                         for i in range(2)]
                ps_t6 = T6(nc.psum_tensor("ps_t6", [128, 1024], BF16))
                ps_w6 = T6(nc.psum_tensor("ps_w6", [128, 1024], BF16))

                if "g2w" in KSKIP:
                    p.add("dve", lambda e: e.memset(g2bc[:, :], 1.0), inc="dve")
                p.add("sync", lambda e: e.dma_start(out=g2row[:, :], in_=g2_r),
                      inc="dsy", inc_n=16) if "g2w" not in KSKIP else None
                g2ld = W("dsy")
                g2evac = 0
                if "g2w" not in KSKIP:
                    p.add("dve", lambda e: e.tensor_copy(g2hl[0:1, 0:D], g2row[0:1, :]),
                          waits=[("dsy", g2ld)], inc="dve")
                    gh1 = W("dve")
                    p.add("dve", lambda e: e.scalar_tensor_tensor(
                        g2hl[0:1, D:2 * D], g2hl[0:1, 0:D], -1.0, g2row[0:1, :],
                        ALU.mult, ALU.add), waits=[("dve", gh1)], inc="dve")
                    g2evac = W("dve")
                for h in range(D // 512 if "g2w" not in KSKIP else 0):
                    p.add("te", lambda e, h=h: e.matmul(
                        ps_o2[0][:, :], onesb[0:1, :],
                        g2hl[0:1, h * 512:(h + 1) * 512],
                        start=True, stop=False),
                        waits=[("dve", g2evac)], inc="te")
                    p.add("te", lambda e, h=h: e.matmul(
                        ps_o2[0][:, :], onesb[0:1, :],
                        g2hl[0:1, D + h * 512:D + (h + 1) * 512],
                        start=False, stop=True), inc="te")
                    tv = W("te")
                    p.add("dve", lambda e, h=h: e.tensor_copy(
                        g2bc[:, h * 512:(h + 1) * 512], ps_o2[0][:, :]),
                        waits=[("te", tv)], inc="dve")
                    g2evac = W("dve")
                g2v = W("dve")
                wt_gate_d = g2v
                wt_gate_a = 0
                for mc in range(DC if "p6" in PH else 0):
                    p.add("sync", lambda e, mc=mc: e.dma_start(
                        out=w6f[:, :], in_=wout_i[mc * 128:(mc + 1) * 128, :]),
                        waits=[("dve", wt_gate_d), ("act", wt_gate_a)],
                        inc="dsy", inc_n=16)
                    lv = W("dsy")
                    p.add("dve", lambda e: e.tensor_copy(w6b[:, :], w6f[:, :]),
                          waits=[("dsy", lv), ("te", W("te"))], inc="dve")
                    cv = W("dve")
                    wt_gate_d = cv
                    for ic in range(DC):
                        p.add("te", lambda e, ic=ic: e.transpose(
                            ps_w6[:, 0:128], w6b[:, ic * 128:(ic + 1) * 128],
                            eyeb[:, :]), waits=[("dve", cv), ("act", wt_gate_a)],
                            inc="te")
                        tv = W("te")
                        p.add("act", lambda e, ic=ic, mc=mc: e.activation(
                            woutT[:, ic * D + mc * 128: ic * D + (mc + 1) * 128],
                            ps_w6[:, 0:128], AF.Copy), waits=[("te", tv)], inc="act")
                        wt_gate_a = W("act")
                wt_done = W("act")

                def rd_attfull(e, ic):
                    rk = e.cc_rank(replica_groups=[list(range(NCORES))])
                    co = cc_out.ap().rearrange("(r p) n -> r p n", p=128)
                    return e.dma_start(out=afull[:, ic * TOWN:(ic + 1) * TOWN],
                                       in_=co[ic, :, bass.ts(rk, TOWN)])
                for ic in range(DC if "p6" in PH else 0):
                    p.add("gps", lambda e, ic=ic: rd_attfull(e, ic),
                          waits=[("cc", 1) if "cc" in PH else ("cc", 0)],
                          inc="dgp", inc_n=16)
                afv = W("dgp")
                if dbg:
                    p.add("gps", lambda e: e.dma_start(out=dbg_afull, in_=afull[:, :]),
                          waits=[("dgp", afv)], inc="dgp", inc_n=16)

                evs = {0: 0, 1: 0}
                tr6_prev = 0
                s_done = 0
                sgate = {0: ("dya", 0), 1: ("dyb", 0)}
                o2rd_d = 0
                o2rd_a = 0
                ogate = 0
                for tbk in range(NB6 if "p6" in PH else 0):
                    t0 = tbk * TB
                    for mc in range(DC):
                        yb = mc % 2
                        for ic in range(DC):
                            w = []
                            if ic == 0:
                                w = [("dgp", afv), ("dve", evs[yb]),
                                     ("act", wt_done), ("dve", ffn_dve)]
                            p.add("te", lambda e, mc=mc, ic=ic, t0=t0, yb=yb: e.matmul(
                                ps_o2[yb][:, 0:TB],
                                woutT[:, ic * D + mc * 128: ic * D + (mc + 1) * 128],
                                afull[:, ic * TOWN + t0: ic * TOWN + t0 + TB],
                                start=(ic == 0), stop=(ic == DC - 1)),
                                waits=w, inc="te")
                        wv = W("te")
                        p.add("dve", lambda e, mc=mc, yb=yb: e.tensor_copy(
                            o2scr[:, mc * TB:(mc + 1) * TB], ps_o2[yb][:, 0:TB]),
                            waits=[("te", wv), ("te", tr6_prev)], inc="dve")
                        evs[yb] = W("dve")
                    o2v = W("dve")
                    if dbg and tbk == 0:
                        p.add("gps", lambda e: e.dma_start(
                            out=dbg_o2[:, 0:DC * TB], in_=o2scr[:, :]),
                            waits=[("dve", o2v)], inc="dgp", inc_n=16)
                        p.add("gps", lambda e: e.dma_start(
                            out=dbg_g2bc, in_=g2bc[:, :]),
                            waits=[("dve", g2v)], inc="dgp", inc_n=16)
                    for tt in range(TB // 128):
                        gt = (t0 + tt * 128) // 128
                        for mc in range(DC):
                            p.add("te", lambda e, mc=mc, tt=tt: e.transpose(
                                ps_t6[:, 0:128],
                                o2scr[:, mc * TB + tt * 128: mc * TB + (tt + 1) * 128],
                                eyeb[:, :]),
                                waits=[("dve", o2v), ("dve", ogate)], inc="te")
                            tv = W("te")
                            p.add("dve", lambda e, mc=mc: e.tensor_copy(
                                o2t[:, mc * 128:(mc + 1) * 128], ps_t6[:, 0:128]),
                                waits=[("te", tv), ("dve", o2rd_d), ("act", o2rd_a)],
                                inc="dve")
                            ogate = W("dve")
                        tr6_prev = W("te")
                        ov = W("dve")
                        if dbg and tbk == 0:
                            p.add("gps", lambda e, tt=tt: e.dma_start(
                                out=dbg_o2t[:, tt * D:(tt + 1) * D], in_=o2t[:, :]),
                                waits=[("dve", ov)], inc="dgp", inc_n=16)
                            o2t_dumped = W("dgp")
                            rdy, _ = ln_chain(o2t[:, :], st6, 0,
                                              [("dve", ov), ("dgp", o2t_dumped)])
                        else:
                            rdy, _ = ln_chain(o2t[:, :], st6, 0, [("dve", ov)])
                        p.add("dve", lambda e: e.tensor_scalar(
                            o2t[:, :], o2t[:, :], st6[:, 2:3], st6[:, 4:5],
                            ALU.add, ALU.mult), waits=[("act", rdy)], inc="dve")
                        lv = W("dve")
                        if dbg:
                            p.add("dve", lambda e, gt=gt: e.tensor_copy(
                                stscr6[:, gt * 8:gt * 8 + 5], st6[:, 0:5]),
                                waits=[("dve", lv)], inc="dve")
                        spg = gt % 2
                        p.add("dve", lambda e, spg=spg: e.tensor_tensor(
                            stile[:, spg * D:(spg + 1) * D], o2t[:, :], g2bc[:, :],
                            ALU.mult),
                            waits=[("dve", lv), sgate[spg]], inc="dve")
                        sv6 = W("dve")
                        o2rd_d = sv6
                        o2rd_a = W("act")
                        ssem = "dya" if spg == 0 else "dyb"
                        p.add("gps", lambda e, gt=gt, spg=spg: e.dma_start(
                            out=s_dram.ap()[gt], in_=stile[:, spg * D:(spg + 1) * D]),
                            waits=[("dve", sv6)], inc=ssem, inc_n=16)
                        sgate[spg] = (ssem, W(ssem))
                        s_done = W("dve")
                if dbg:
                    p.add("gps", lambda e: e.dma_start(out=dbg_st, in_=stscr6[:, :]),
                          waits=[("dve", W("dve"))], inc="dgp", inc_n=16)

            p.phase = "p7"
            p.barrier()
            with ExitStack() as sc7:
                T7 = sc7.enter_context
                gsil = T7(nc.sbuf_tensor("gsil", [128, DC * TOWN], BF16))
                gtile = T7(nc.sbuf_tensor("gtile", [128, D], BF16))
                xtl = T7(nc.sbuf_tensor("xtl", [128, 2 * D], F32))
                ytl = T7(nc.sbuf_tensor("ytl", [128, 2 * D], F32))
                st7 = T7(nc.sbuf_tensor("st7", [128, 2 * D], BF16))
                ps_g = T7(nc.psum_tensor("ps_g", [128, 1024], BF16))

                if dbg:
                    p.add("gps", lambda e: e.dma_start(out=dbg_g, in_=gacc[:, :]),
                          waits=[("dve", ffn_dve), ("act", ffn_act),
                                 ("dve", s_done)], inc="dgp", inc_n=16)
                    p.add("gps", lambda e: e.dma_start(
                        out=dbg_s.rearrange("p (t d) -> t p d", d=D),
                        in_=s_dram.ap()[:, :, :]), inc="dgp", inc_n=16)
                if "silu" in KSKIP:
                    for dc in range(DC):
                        p.add("dve", lambda e, dc=dc: e.tensor_copy(
                            gsil[:, dc * TOWN:(dc + 1) * TOWN],
                            gacc[:, dc * TOWN:(dc + 1) * TOWN]),
                            waits=[("dve", ffn_dve), ("act", ffn_act)], inc="dve")
                    silv = W("act")
                else:
                    for dc in range(DC):
                        p.add("act", lambda e, dc=dc: e.activation(
                            gsil[:, dc * TOWN:(dc + 1) * TOWN],
                            gacc[:, dc * TOWN:(dc + 1) * TOWN], AF.Silu),
                            waits=[("dve", ffn_dve), ("act", ffn_act)], inc="act")
                    silv = W("act")

                xo_t = x_own.rearrange("(t p) d -> t p d", p=128)
                yo_t = y_own.rearrange("(t p) d -> t p d", p=128)
                xread = {0: 0, 1: 0}
                ystore = {0: 0, 1: 0}
                ggate = 0
                ylast = 0
                for t in range(OTILE):
                    pg = t % 2
                    xsem = "dsa" if pg == 0 else "dsb"
                    lsem = "ds0" if pg == 0 else "ds1"
                    p.add("sync", lambda e, t=t, pg=pg: e.dma_start(
                        out=xtl[:, pg * D:(pg + 1) * D], in_=xo_t[t]),
                        waits=[("dve", xread[pg])], inc=xsem, inc_n=16)
                    xv7 = W(xsem)
                    if "p6" in PH:
                        p.add("sync", lambda e, t=t, pg=pg: e.dma_start(
                            out=st7[:, pg * D:(pg + 1) * D], in_=s_dram.ap()[t]),
                            inc=lsem, inc_n=16)
                    else:
                        p.add("dve", lambda e, pg=pg: e.memset(
                            st7[:, pg * D:(pg + 1) * D], 1.0), inc="dve")
                    lx7 = W(lsem)
                    for dc in range(DC):
                        p.add("te", lambda e, dc=dc, t=t: e.transpose(
                            ps_g[:, 0:128],
                            gsil[:, dc * TOWN + t * 128: dc * TOWN + (t + 1) * 128],
                            eyeb[:, :]),
                            waits=[("act", silv), ("dve", ggate)], inc="te")
                        tv = W("te")
                        p.add("dve", lambda e, dc=dc: e.tensor_copy(
                            gtile[:, dc * 128:(dc + 1) * 128], ps_g[:, 0:128]),
                            waits=[("te", tv)], inc="dve")
                        ggate = W("dve")
                    gv = W("dve")
                    ysem = "dya" if pg == 0 else "dyb"
                    p.add("dve", lambda e, t=t, pg=pg: e.tensor_tensor(
                        ytl[:, pg * D:(pg + 1) * D], st7[:, pg * D:(pg + 1) * D],
                        gtile[:, :], ALU.mult),
                        waits=[("dve", gv), (lsem, lx7), (ysem, ystore[pg])],
                        inc="dve")
                    m1 = W("dve")
                    p.add("dve", lambda e, t=t, pg=pg: e.tensor_tensor(
                        ytl[:, pg * D:(pg + 1) * D], ytl[:, pg * D:(pg + 1) * D],
                        xtl[:, pg * D:(pg + 1) * D], ALU.add),
                        waits=[("dve", m1), (xsem, xv7)], inc="dve")
                    yv = W("dve")
                    xread[pg] = yv
                    p.add("gps", lambda e, t=t, pg=pg: e.dma_start(
                        out=yo_t[t], in_=ytl[:, pg * D:(pg + 1) * D]),
                        waits=[("dve", yv)], inc=ysem, inc_n=16)
                    ystore[pg] = W(ysem)
                p.add("gps", lambda e: e.nop(),
                      waits=[("dya", W("dya")), ("dyb", W("dyb"))])

        with ExitStack() as es:
            sems = {s: es.enter_context(nc.semaphore(s)) for s in SEMS}
            blk = es.enter_context(nc.Block())
            blk.sync(lambda e: p.emit("sync", e, sems))
            blk.gpsimd(lambda e: p.emit("gps", e, sems))
            blk.tensor(lambda e: p.emit("te", e, sems))
            blk.scalar(lambda e: p.emit("act", e, sems))
            blk.vector(lambda e: p.emit("dve", e, sems))
        pre.close()

    return nc



def make_consts():
    eye_b = np.eye(128, dtype=ml_dtypes.bfloat16)
    eye_f = np.eye(128, dtype=np.float32)
    ones_f = np.ones((128, 128), dtype=np.float32)
    qss = np.zeros((128, 2), dtype=np.float32)
    qss[0:64, 0] = 1.0
    qss[64:128, 1] = 1.0
    qss = qss.astype(ml_dtypes.bfloat16)
    bc2 = np.zeros((2, 128), dtype=np.float32)
    bc2[0, 0:64] = 1.0
    bc2[1, 64:128] = 1.0
    ohot = np.zeros((E, E * 128), dtype=np.float32)
    for ee in range(E):
        ohot[ee, ee * 128:(ee + 1) * 128] = 1.0
    masks = np.zeros((128, RPQ * QB), dtype=np.float32)
    for r in range(RPQ):
        kp = np.arange(128)[:, None]
        ql = np.arange(QB)[None, :]
        masks[:, r * QB:(r + 1) * QB] = (kp + 128 * r <= ql).astype(np.float32)
    return (eye_b, eye_f, ones_f, qss, bc2, ohot.astype(ml_dtypes.bfloat16),
            masks.astype(ml_dtypes.bfloat16))


def make_in_maps(S, args):
    (x, g1, Wq, Wkv, Wout, g2, rW, rb, sW1, sb1, sW2, sb2,
     eW1, eb1, eW2, eb2) = args
    B = 2
    N = B * S
    TOWN = N // NCORES
    f32 = lambda a: np.ascontiguousarray(a, dtype=np.float32)
    xf = f32(x).reshape(N, D)
    eye_b, eye_f, ones_f, qss, bc2, ohot, masks = make_consts()
    g1c = f32(g1).reshape(DC, 128).T.copy()
    sb1c = f32(sb1).reshape(HC, 128).T.copy()
    sb2c = f32(sb2).reshape(DC, 128).T.copy()
    eb1c = f32(eb1).reshape(E, HC, 128).transpose(2, 0, 1).reshape(128, E * HC).copy()
    rwc = f32(rW).reshape(DC, 128, E).transpose(1, 0, 2).reshape(128, DC * E).copy()
    shared = {
        "x_full": xf,
        "wkv": f32(Wkv), "wout": f32(Wout),
        "g1_c": g1c, "g2_r": f32(g2).reshape(1, D),
        "rw_c": rwc, "rb_r": f32(rb).reshape(1, E),
        "sw1": f32(sW1), "sw2": f32(sW2),
        "sb1_c": sb1c, "sb2_c": sb2c,
        "ew1": f32(eW1), "ew2": f32(eW2),
        "eb1_c": eb1c, "eb2_t": f32(eb2),
        "eye_b": eye_b, "eye_f": eye_f, "ones_f": ones_f,
        "qss_t": qss, "bc2_t": bc2, "ohot_i": ohot, "masks": masks,
    }
    in_maps = []
    for c in range(NCORES):
        m = dict(shared)
        m["x_own"] = xf[c * TOWN:(c + 1) * TOWN].copy()
        m["wq_my"] = f32(Wq)[c * 128:(c + 1) * 128, :].copy()
        in_maps.append(m)
    return in_maps


def kernel(L0_attn_x, L0_attn_g1, L0_attn_Wq, L0_attn_Wkv, L0_attn_Wout,
           L0_attn_g2, L0_ffn_router_W, L0_ffn_router_b, L0_ffn_sW1, L0_ffn_sb1,
           L0_ffn_sW2, L0_ffn_sb2, L0_ffn_eW1, L0_ffn_eb1, L0_ffn_eW2,
           L0_ffn_eb2):
    x = np.asarray(L0_attn_x)
    B_, S_, D_ = x.shape
    assert B_ == 2 and D_ == D
    args = (x, L0_attn_g1, L0_attn_Wq, L0_attn_Wkv, L0_attn_Wout, L0_attn_g2,
            L0_ffn_router_W, L0_ffn_router_b, L0_ffn_sW1, L0_ffn_sb1,
            L0_ffn_sW2, L0_ffn_sb2, L0_ffn_eW1, L0_ffn_eb1, L0_ffn_eW2,
            L0_ffn_eb2)
    nc = build_nc(S_)
    in_maps = make_in_maps(S_, [np.asarray(a) for a in args])
    res = run_bass_kernel_spmd(nc, in_maps, core_ids=list(range(NCORES)))
    y = np.concatenate([res.results[c]["y_own"] for c in range(NCORES)], axis=0)
    return y.reshape(B_, S_, D_).astype(np.float32)


# version 14 (final)
# speedup vs baseline: 1.0120x; 1.0120x over previous
import numpy as np
import ml_dtypes
from contextlib import ExitStack

import concourse.bass as bass
import concourse.mybir as mybir
from concourse.bass_utils import run_bass_kernel_spmd

F32 = mybir.dt.float32
BF16 = mybir.dt.bfloat16
AF = mybir.ActivationFunctionType
ALU = mybir.AluOpType
AX = mybir.AxisListType

NCORES = 8
D = 1024
NH = 16
DH = 64
HPC = NH // NCORES
E = 15
HF = 1024
KC = 128
QB = 512
DC = D // 128
HC = HF // 128
RPQ = QB // KC
LN4 = 1.3862943611198906


import os as _os0
KSAFE = _os0.environ.get("KSAFE", "0") == "1"
KSAFE_PH = set(_os0.environ.get("KSAFE_PH", "").split(",")) - {""}
_PHASE_MAP = {}
_DESC_MAP = {}


class Prog:

    def __init__(self):
        self.ops = []
        self.cnt = {}
        self.pending = None
        self.stamped = set()
        self.phase = "p0"

    def v(self, sem):
        return self.cnt.get(sem, 0)

    def barrier(self):
        self.pending = {s: self.cnt.get(s, 0)
                        for s in ("te", "act", "dve", "dsy", "dgp", "dsa", "dsb",
                                  "dg0", "dg1", "dg2", "dg3", "dya", "dyb",
                                  "ds0", "ds1", "wr0", "wr1", "wr2")}
        self.stamped = set()

    def add(self, eng, fn, waits=(), inc=None, inc_n=1):
        waits = list(waits)
        if KSAFE or self.phase in KSAFE_PH:
            waits += [(sname, self.cnt.get(sname, 0))
                      for sname in ("te", "act", "dve", "dsy", "dgp", "dsa", "dsb",
                                    "dg0", "dg1", "dg2", "dg3", "dya", "dyb",
                                    "ds0", "ds1", "wr0", "wr1", "wr2")]
        if self.pending is not None and eng not in self.stamped:
            waits += [(s, v) for s, v in self.pending.items() if v > 0]
            self.stamped.add(eng)
        if inc is not None:
            self.cnt[inc] = self.cnt.get(inc, 0) + (1 if inc_n is None else inc_n)
        self.ops.append((eng, fn, tuple(waits), inc, inc_n, self.phase))
        return self.cnt.get(inc, 0) if inc is not None else None

    def emit(self, name, e, sems):
        last = {}
        for op_rec in self.ops:
            eng_name, fn, waits, inc, inc_n = op_rec[:5]
            if eng_name != name:
                continue
            for s, val in waits:
                if val is None or val <= 0:
                    continue
                if last.get(s, -1) >= val:
                    continue
                e.wait_ge(sems[s], val)
                last[s] = val
            ins = fn(e)
            try:
                _PHASE_MAP[ins.ins.name] = op_rec[5]
                _DESC_MAP[ins.ins.name] = (name, waits, inc)
            except Exception:
                pass
            if inc is not None:
                if inc_n is None:
                    ins.then_inc(sems[inc])
                else:
                    ins.then_inc(sems[inc], inc_n)


def build_nc(S, dbg=False, phases="all"):
    B = 2
    N = B * S
    TOWN = N // NCORES
    OTILE = TOWN // 128
    NQB = S // QB
    TB = min(512, TOWN)
    NTB = TOWN // TB
    TGRP = 2 if NTB % 2 == 0 else 1
    NGRP = NTB // TGRP
    HTOK = TGRP * TB
    NB6 = TOWN // TB

    import os as _os2
    nc = bass.Bass("TRN2", target_bir_lowering=False,
                   detect_race_conditions=_os2.environ.get("KRACE", "0") == "1")

    def inp(name, shape, dtype=F32):
        return nc.dram_tensor(name, shape, dtype, kind="ExternalInput").ap()

    x_own = inp("x_own", [TOWN, D])
    x_full = inp("x_full", [N, D])
    wq_my = inp("wq_my", [128, D])
    wkv_i = inp("wkv", [2 * DH, D])
    wout_i = inp("wout", [D, D])
    g1_c = inp("g1_c", [128, DC])
    g2_r = inp("g2_r", [1, D])
    rw_c = inp("rw_c", [128, DC * E])
    rb_r = inp("rb_r", [1, E])
    sw1_i = inp("sw1", [D, HF])
    sw2_i = inp("sw2", [HF, D])
    sb1_c = inp("sb1_c", [128, HC])
    sb2_c = inp("sb2_c", [128, DC])
    ew1_i = inp("ew1", [E, D, HF])
    ew2_i = inp("ew2", [E, HF, D])
    eb1_c = inp("eb1_c", [128, E * HC])
    eb2_i = inp("eb2_t", [E, D])
    eye_bi = inp("eye_b", [128, 128], BF16)
    eye_fi = inp("eye_f", [128, 128], F32)
    ones_i = inp("ones_f", [128, 128], F32)
    qss_i = inp("qss_t", [128, 2], BF16)
    bc2_i = inp("bc2_t", [2, 128], F32)
    ohot_i = inp("ohot_i", [E, E * 128], BF16)
    mask_i = inp("masks", [128, RPQ * QB], BF16)

    y_own = nc.dram_tensor("y_own", [TOWN, D], F32, kind="ExternalOutput").ap()

    cc_in = nc.dram_tensor("cc_in", [128, N], BF16)
    s_dram = nc.dram_tensor("s_dram", [OTILE, 128, D], BF16)
    cc_out = nc.dram_tensor("cc_out", [NCORES * 128, N], BF16)
    if dbg:
        dbg_k = nc.dram_tensor("dbg_k", [128, N], BF16, kind="ExternalOutput").ap()
        dbg_q = nc.dram_tensor("dbg_q", [128, N], BF16, kind="ExternalOutput").ap()
        dbg_v = nc.dram_tensor("dbg_v", [128, (N // KC) * (DH + 1)], BF16, kind="ExternalOutput").ap()
        dbg_xnT = nc.dram_tensor("dbg_xnT", [128, DC * TOWN], BF16, kind="ExternalOutput").ap()
        dbg_cw = nc.dram_tensor("dbg_cw", [128, (TOWN // 128) * 16], F32, kind="ExternalOutput").ap()
        dbg_att = nc.dram_tensor("dbg_att", [128, N], BF16, kind="ExternalOutput").ap()
        dbg_afull = nc.dram_tensor("dbg_afull", [128, DC * TOWN], BF16, kind="ExternalOutput").ap()
        dbg_g = nc.dram_tensor("dbg_g", [128, DC * TOWN], F32, kind="ExternalOutput").ap()
        dbg_s = nc.dram_tensor("dbg_s", [128, (TOWN // 128) * D], BF16, kind="ExternalOutput").ap()
        dbg_o2 = nc.dram_tensor("dbg_o2", [128, DC * 512], BF16, kind="ExternalOutput").ap()
        dbg_g2bc = nc.dram_tensor("dbg_g2bc", [128, D], F32, kind="ExternalOutput").ap()
        dbg_st = nc.dram_tensor("dbg_st", [128, (TOWN // 128) * 8], F32, kind="ExternalOutput").ap()
        dbg_o2t = nc.dram_tensor("dbg_o2t", [128, 4 * D], F32, kind="ExternalOutput").ap()

    import os as _os
    KSKIP = set(_os.environ.get("KSKIP", "").split(","))
    p = Prog()
    W = p.v
    PH = set(phases.split(",")) if phases != "all" else {"p1", "p2", "p3", "cc", "p5", "p6"}
    SEMS = ["dsy", "dgp", "te", "act", "dve", "cc", "dsa", "dsb", "dg0", "dg1", "dg2", "dg3", "dya", "dyb", "ds0", "ds1", "wr0", "wr1", "wr2"]

    with ExitStack() as top:
        T = top.enter_context

        eyeb = T(nc.sbuf_tensor("eyeb", [128, 128], BF16))
        eyef = T(nc.sbuf_tensor("eyef", [128, 128], F32))
        ones = T(nc.sbuf_tensor("ones", [128, 128], F32))
        onesb = T(nc.sbuf_tensor("onesb", [128, 128], BF16))
        qssT = T(nc.sbuf_tensor("qssT", [128, 2], BF16))
        bc2b = T(nc.sbuf_tensor("bc2b", [2, 128], BF16))
        bc2T = T(nc.sbuf_tensor("bc2T", [2, 128], F32))


        sb1c = T(nc.sbuf_tensor("sb1c", [128, HC], F32))
        sb2c = T(nc.sbuf_tensor("sb2c", [128, DC], F32))
        eb1c = T(nc.sbuf_tensor("eb1c", [128, E * HC], F32))
        eb2t = T(nc.sbuf_tensor("eb2t", [E, D], BF16))

        xnT = T(nc.sbuf_tensor("xnT", [128, DC * TOWN], BF16))
        cwb = T(nc.sbuf_tensor("cwb", [128, OTILE * 16], F32))
        cln4 = T(nc.sbuf_tensor("cln4", [128, 1], F32))
        cnhalf = T(nc.sbuf_tensor("cnhalf", [128, 1], F32))
        ceps = T(nc.sbuf_tensor("ceps", [128, 1], F32))

        pre = ExitStack()
        Tp = pre.enter_context
        g1c = Tp(nc.sbuf_tensor("g1c", [128, DC], F32))
        rwc = Tp(nc.sbuf_tensor("rwc", [128, DC * E], F32))
        rwch = Tp(nc.sbuf_tensor("rwch", [128, DC * E], BF16))
        rwcl = Tp(nc.sbuf_tensor("rwcl", [128, DC * E], BF16))
        rbbc = Tp(nc.sbuf_tensor("rbbc", [128, E], F32))
        rbrow = Tp(nc.sbuf_tensor("rbrow", [1, E], F32))
        wqT = Tp(nc.sbuf_tensor("wqT", [128, DC * 128], BF16))
        wkvT = Tp(nc.sbuf_tensor("wkvT", [128, DC * 128], BF16))
        sqscr = Tp(nc.sbuf_tensor("sqscr", [128, D], BF16))

        for dst, src in [
            (eyeb[:, :], eye_bi), (eyef[:, :], eye_fi), (ones[:, :], ones_i),
            (qssT[:, :], qss_i), (bc2T[:, :], bc2_i),
            (g1c[:, :], g1_c), (sb1c[:, :], sb1_c), (sb2c[:, :], sb2_c),
            (eb1c[:, :], eb1_c), (rwc[:, :], rw_c),
            (rbrow[:, :], rb_r),
        ]:
            p.add("sync", lambda e, d=dst, s=src: e.dma_start(out=d, in_=s),
                  inc="dsy", inc_n=16)
        ld0 = W("dsy")

        p.add("dve", lambda e: e.tensor_copy(onesb[:, :], ones[:, :]),
              waits=[("dsy", ld0)], inc="dve")
        p.add("dve", lambda e: e.tensor_copy(bc2b[:, :], bc2T[:, :]),
              waits=[("dsy", ld0)], inc="dve")
        p.add("dve", lambda e: e.tensor_copy(rwch[:, :], rwc[:, :]),
              waits=[("dsy", ld0)], inc="dve")
        rwv = W("dve")
        p.add("dve", lambda e: e.scalar_tensor_tensor(
            rwcl[:, :], rwch[:, :], -1.0, rwc[:, :], ALU.mult, ALU.add),
            waits=[("dve", rwv)], inc="dve")
        p.add("dve", lambda e: e.memset(cln4[:, :], LN4), inc="dve")
        p.add("dve", lambda e: e.memset(cwb[:, :], 0.0), inc="dve")
        p.add("dve", lambda e: e.memset(cnhalf[:, :], -0.5), inc="dve")
        p.add("dve", lambda e: e.memset(ceps[:, :], 1e-5), inc="dve")

        with nc.sbuf_tensor("wscr_f", [128, D], F32) as wscr_f, \
             nc.sbuf_tensor("wscr_b", [128, D], BF16) as wscr_b, \
             nc.psum_tensor("ps0a", [128, 512], F32) as ps0a, \
             nc.psum_tensor("ps0b", [128, 1024], BF16) as ps0b:

            if "rbbc" not in KSKIP:
                p.add("dve", lambda e: e.tensor_copy(wscr_b[0:1, 0:E], rbrow[0:1, :]),
                      waits=[("dsy", ld0), ("dve", W("dve"))], inc="dve")
                hv = W("dve")
                p.add("dve", lambda e: e.scalar_tensor_tensor(
                    wscr_b[0:1, 16:16 + E], wscr_b[0:1, 0:E], -1.0, rbrow[0:1, :],
                    ALU.mult, ALU.add), waits=[("dve", hv)], inc="dve")
                lv0 = W("dve")
                p.add("te", lambda e: e.matmul(
                    ps0a[:, 0:E], onesb[0:1, :], wscr_b[0:1, 0:E],
                    start=True, stop=False), waits=[("dve", lv0)], inc="te")
                p.add("te", lambda e: e.matmul(
                    ps0a[:, 0:E], onesb[0:1, :], wscr_b[0:1, 16:16 + E],
                    start=False, stop=True), inc="te")
                p.add("dve", lambda e: e.tensor_copy(rbbc[:, :], ps0a[:, 0:E]),
                      waits=[("te", W("te"))], inc="dve")

            def prep_T(src_ap, nrow, dst, dst_col_fn, scale_fn):
                p.add("sync", lambda e, s=src_ap, n=nrow: e.dma_start(
                    out=wscr_f[0:n, :], in_=s),
                    waits=[("dve", W("dve")), ("act", W("act")), ("te", W("te"))],
                    inc="dsy", inc_n=16)
                lv = W("dsy")
                p.add("dve", lambda e, n=nrow: e.tensor_copy(
                    wscr_b[0:n, :], wscr_f[0:n, :]), waits=[("dsy", lv)], inc="dve")
                cv = W("dve")
                for dc in range(DC):
                    p.add("te", lambda e, dc=dc, n=nrow: e.transpose(
                        ps0b[:, 0:n], wscr_b[0:n, dc * 128:(dc + 1) * 128],
                        eyeb[0:n, 0:n]),
                        waits=[("dve", cv), ("act", W("act"))], inc="te")
                    tv = W("te")
                    sc = scale_fn(dc)
                    if sc is None:
                        p.add("act", lambda e, dc=dc, n=nrow: e.activation(
                            dst[:, dst_col_fn(dc):dst_col_fn(dc) + n],
                            ps0b[:, 0:n], AF.Copy), waits=[("te", tv)], inc="act")
                    else:
                        p.add("act", lambda e, dc=dc, n=nrow, sc=sc: e.activation(
                            dst[:, dst_col_fn(dc):dst_col_fn(dc) + n],
                            ps0b[:, 0:n], AF.Copy, scale=sc),
                            waits=[("te", tv)], inc="act")

            if "eb2" not in KSKIP:
                p.add("sync", lambda e: e.dma_start(out=wscr_f[0:E, :], in_=eb2_i),
                      inc="dsy", inc_n=16)
                eb2ld = W("dsy")
                p.add("dve", lambda e: e.tensor_copy(eb2t[:, :], wscr_f[0:E, :]),
                      waits=[("dsy", eb2ld)], inc="dve")
            if "prept" not in KSKIP:
                prep_T(wq_my, 128, wqT, lambda dc: dc * 128, lambda dc: g1c[:, dc:dc + 1])
                prep_T(wkv_i, 128, wkvT, lambda dc: dc * 128, lambda dc: g1c[:, dc:dc + 1])

        def ln_chain(xt_ap, st, c0, wx):
            s1, ss = st[:, c0:c0 + 1], st[:, c0 + 1:c0 + 2]
            nm, var = st[:, c0 + 2:c0 + 3], st[:, c0 + 3:c0 + 4]
            rstd = st[:, c0 + 4:c0 + 5]
            p.add("dve", lambda e: e.reduce_sum(s1, xt_ap, axis=AX.X),
                  waits=list(wx), inc="dve")
            rv = W("dve")
            p.add("act", lambda e: e.activation(sqscr[:, :], xt_ap, AF.Square,
                                                accum_out=ss),
                  waits=list(wx), inc="act")
            qv = W("act")
            p.add("dve", lambda e: e.tensor_scalar(nm, s1, -1.0 / D, None, ALU.mult),
                  waits=[("dve", rv)], inc="dve")
            nv = W("dve")
            p.add("dve", lambda e: e.tensor_tensor(s1, nm, nm, ALU.mult),
                  waits=[("dve", nv)], inc="dve")
            mv = W("dve")
            p.add("dve", lambda e: e.tensor_scalar(var, ss, 1.0 / D, None, ALU.mult),
                  waits=[("act", qv), ("dve", mv)], inc="dve")
            vv = W("dve")
            p.add("dve", lambda e: e.tensor_tensor(var, var, s1, ALU.subtract),
                  waits=[("dve", vv)], inc="dve")
            vv2 = W("dve")
            p.add("act", lambda e: e.activation(rstd, var, AF.Ln, bias=ceps[:, 0:1]),
                  waits=[("dve", vv2)], inc="act")
            lv = W("act")
            p.add("act", lambda e: e.activation(rstd, rstd, AF.Exp,
                                                scale=cnhalf[:, 0:1]),
                  waits=[("act", lv)], inc="act")
            return W("act"), nv

        with ExitStack() as sc_pa:
            Tp = sc_pa.enter_context
            p.barrier()
            maskb = Tp(nc.sbuf_tensor("maskb", [128, RPQ * QB], BF16))
            p.add("sync", lambda e: e.dma_start(out=maskb[:, :], in_=mask_i),
                  inc="dsy", inc_n=16)
            kbuf = Tp(nc.sbuf_tensor("kbuf", [128, N], BF16))
            vbuf = Tp(nc.sbuf_tensor("vbuf", [128, (N // KC) * (DH + 1)], BF16))
            qhT = Tp(nc.sbuf_tensor("qhT", [128, N], BF16))

            with ExitStack() as sc_p:
                T1 = sc_p.enter_context

                with ExitStack() as s1s:
                    T1a = s1s.enter_context
                    xt2 = T1a(nc.sbuf_tensor("xt2", [128, 2 * D], F32))
                    xnt = T1a(nc.sbuf_tensor("xnt", [128, D], F32))
                    st8 = T1a(nc.sbuf_tensor("st8", [128, 12], F32))
                    xnh = T1a(nc.sbuf_tensor("xnh", [128, D], BF16))
                    xnl = T1a(nc.sbuf_tensor("xnl", [128, D], BF16))
                    xnTl = T1a(nc.sbuf_tensor("xnTl", [128, D], BF16))
                    lgs = T1a(nc.sbuf_tensor("lgs", [128, 16], F32))
                    lgs2 = T1a(nc.sbuf_tensor("lgs2", [128, 16], F32))
                    msk = T1a(nc.sbuf_tensor("msk", [128, 2 * 16], F32))
                    lgs3 = T1a(nc.sbuf_tensor("lgs3", [128, 4], F32))
                    ps_t = T1a(nc.psum_tensor("ps_t", [128, D], BF16))
                    ps_tl = T1a(nc.psum_tensor("ps_tl", [128, D], BF16))
                    ps_lg = T1a(nc.psum_tensor("ps_lg", [128, 512], F32))

                    xo_t = x_own.rearrange("(t p) d -> t p d", p=128)
                    p.phase = "p1"
                    p.barrier()
                    xgate = {0: [], 1: []}
                    lgv_prev = 0
                    trv_prev = 0
                    for t in range(OTILE if "p1" in PH else 0):
                        pg = t % 2
                        xsem = "dsa" if pg == 0 else "dsb"
                        xt = xt2[:, pg * D:(pg + 1) * D]
                        p.add("sync", lambda e, t=t, xt=xt: e.dma_start(out=xt, in_=xo_t[t]),
                              waits=xgate[pg], inc=xsem, inc_n=16)
                        lx = W(xsem)
                        rdy, _ = ln_chain(xt, st8, 0, [(xsem, lx)])
                        p.add("dve", lambda e, xt=xt: e.tensor_scalar(
                            xnt[:, :], xt, st8[:, 2:3], st8[:, 4:5], ALU.add, ALU.mult),
                            waits=[("act", rdy), ("te", trv_prev)], inc="dve")
                        xnv = W("dve")
                        xgate[pg] = [("dve", xnv), ("act", W("act"))]
                        p.add("dve", lambda e: e.tensor_copy(xnh[:, :], xnt[:, :]),
                              waits=[("dve", xnv), ("te", lgv_prev)], inc="dve")
                        hv = W("dve")
                        p.add("dve", lambda e: e.scalar_tensor_tensor(
                            xnl[:, :], xnh[:, :], -1.0, xnt[:, :], ALU.mult, ALU.add),
                            waits=[("dve", hv)], inc="dve")
                        lv2 = W("dve")
                        for dc in range(DC):
                            p.add("te", lambda e, dc=dc: e.transpose(
                                ps_t[:, dc * 128:(dc + 1) * 128],
                                xnh[:, dc * 128:(dc + 1) * 128], eyeb[:, :]),
                                waits=[("dve", hv), ("act", W("act")),
                                       ("dve", W("dve"))], inc="te")
                        for dc in range(DC):
                            p.add("te", lambda e, dc=dc: e.transpose(
                                ps_tl[:, dc * 128:(dc + 1) * 128],
                                xnl[:, dc * 128:(dc + 1) * 128], eyeb[:, :]),
                                waits=[("dve", lv2)], inc="te")
                        trv = W("te")
                        trv_prev = trv
                        p.add("act", lambda e, t=t: e.activation(
                            xnT[:, :].rearrange("p (c n) -> p c n", c=DC)
                            [:, :, t * 128:(t + 1) * 128],
                            ps_t[:, :].rearrange("p (c n) -> p c n", c=DC),
                            AF.Copy), waits=[("te", trv)], inc="act")
                        p.add("dve", lambda e: e.tensor_copy(xnTl[:, :], ps_tl[:, :]),
                              waits=[("te", trv)], inc="dve")
                        xfv = W("dve")
                        xTh = xnT[:, :].rearrange("p (c n) -> p c n", c=DC)
                        afv1 = W("act")
                        for dc in range(DC):
                            p.add("te", lambda e, dc=dc, t=t: e.matmul(
                                ps_lg[:, 0:E],
                                xTh[:, dc, t * 128:(t + 1) * 128],
                                rwch[:, dc * E:(dc + 1) * E],
                                start=(dc == 0), stop=False),
                                waits=[("dve", xfv), ("act", afv1)], inc="te")
                            p.add("te", lambda e, dc=dc, t=t: e.matmul(
                                ps_lg[:, 0:E],
                                xTh[:, dc, t * 128:(t + 1) * 128],
                                rwcl[:, dc * E:(dc + 1) * E],
                                start=False, stop=False), inc="te")
                            p.add("te", lambda e, dc=dc: e.matmul(
                                ps_lg[:, 0:E],
                                xnTl[:, dc * 128:(dc + 1) * 128],
                                rwch[:, dc * E:(dc + 1) * E],
                                start=False, stop=(dc == DC - 1)), inc="te")
                        lgv = W("te")
                        lgv_prev = lgv
                        cw_t = cwb[:, t * 16:t * 16 + E]
                        p.add("dve", lambda e: e.tensor_tensor(
                            lgs[:, 0:E], ps_lg[:, 0:E], rbbc[:, :], ALU.add),
                            waits=[("te", lgv)], inc="dve")
                        av = W("dve")
                        p.add("dve", lambda e: e.reduce_max(
                            lgs3[:, 0:1], lgs[:, 0:E], axis=AX.X, negate=True),
                            waits=[("dve", av)], inc="dve")
                        mv = W("dve")
                        p.add("act", lambda e: e.activation(
                            lgs2[:, 0:E], lgs[:, 0:E], AF.Exp, bias=lgs3[:, 0:1]),
                            waits=[("dve", mv)], inc="act")
                        ev = W("act")
                        p.add("dve", lambda e: e.reduce_sum(
                            lgs3[:, 1:2], lgs2[:, 0:E], axis=AX.X),
                            waits=[("act", ev)], inc="dve")
                        sv = W("dve")
                        p.add("dve", lambda e: e.reciprocal(lgs3[:, 2:3], lgs3[:, 1:2]),
                              waits=[("dve", sv)], inc="dve")
                        rv = W("dve")
                        p.add("dve", lambda e: e.tensor_scalar(
                            lgs2[:, 0:E], lgs2[:, 0:E], lgs3[:, 2:3], None, ALU.mult),
                            waits=[("dve", rv)], inc="dve")
                        pv = W("dve")
                        p.add("dve", lambda e: e.tensor_copy(lgs[:, 0:E], lgs2[:, 0:E]),
                              waits=[("dve", pv)], inc="dve")
                        wv = W("dve")
                        for k in range(3):
                            p.add("dve", lambda e: e.reduce_max(
                                lgs3[:, 3:4], lgs[:, 0:E], axis=AX.X),
                                waits=[("dve", wv)], inc="dve")
                            kvv = W("dve")
                            p.add("dve", lambda e: e.tensor_scalar(
                                msk[:, 0:E], lgs[:, 0:E], lgs3[:, 3:4], None,
                                ALU.is_equal), waits=[("dve", kvv)], inc="dve")
                            mk = W("dve")
                            if k == 0:
                                p.add("dve", lambda e, cw_t=cw_t: e.tensor_tensor(
                                    cw_t, msk[:, 0:E], lgs2[:, 0:E], ALU.mult),
                                    waits=[("dve", mk)], inc="dve")
                            else:
                                p.add("dve", lambda e: e.tensor_tensor(
                                    msk[:, 16:16 + E], msk[:, 0:E], lgs2[:, 0:E],
                                    ALU.mult), waits=[("dve", mk)], inc="dve")
                                tv2 = W("dve")
                                p.add("dve", lambda e, cw_t=cw_t: e.tensor_tensor(
                                    cw_t, cw_t, msk[:, 16:16 + E], ALU.add),
                                    waits=[("dve", tv2)], inc="dve")
                            cwv = W("dve")
                            if k < 2:
                                p.add("dve", lambda e: e.scalar_tensor_tensor(
                                    lgs[:, 0:E], msk[:, 0:E], -1e30, lgs[:, 0:E],
                                    ALU.mult, ALU.add), waits=[("dve", cwv)], inc="dve")
                                wv = W("dve")
                    cw_done = W("dve")

                with ExitStack() as s2s:
                    T2 = s2s.enter_context
                    x4 = [T2(nc.sbuf_tensor(f"x4_{i}", [128, 4 * D], F32))
                          for i in range(2)]
                    xn4 = T2(nc.sbuf_tensor("xn4", [128, 4 * D], F32))
                    st2x = T2(nc.sbuf_tensor("st2x", [128, 24], F32))
                    xn2b = T2(nc.sbuf_tensor("xn2b", [128, D], BF16))
                    x2Tb = T2(nc.sbuf_tensor("x2Tb", [128, DC * 512], BF16))
                    kvsc = T2(nc.sbuf_tensor("kvsc", [128, 512], BF16))
                    qksq = T2(nc.sbuf_tensor("qksq", [128, 512], F32))
                    qksqh = T2(nc.sbuf_tensor("qksqh", [128, 512], BF16))
                    qksql = T2(nc.sbuf_tensor("qksql", [128, 512], BF16))
                    ss2 = T2(nc.sbuf_tensor("ss2", [2, 512], F32))
                    ss1 = T2(nc.sbuf_tensor("ss1", [1, 512], F32))
                    ssh = T2(nc.sbuf_tensor("ssh", [2, 2 * 512], BF16))
                    ssl = T2(nc.sbuf_tensor("ssl", [2, 2 * 512], BF16))
                    rbq = T2(nc.sbuf_tensor("rbq", [128, 512], F32))
                    rbk = T2(nc.sbuf_tensor("rbk", [64, 512], F32))
                    ps_tb = T2(nc.psum_tensor("ps_tb", [128, D], BF16))
                    ps_q = T2(nc.psum_tensor("ps_q", [128, 512], F32))
                    ps_kv = T2(nc.psum_tensor("ps_kv", [128, 512], F32))
                    ps_s2 = T2(nc.psum_tensor("ps_s2", [2, 512], F32))
                    ps_s1 = T2(nc.psum_tensor("ps_s1", [1, 512], F32))
                    ps_bc = T2(nc.psum_tensor("ps_bc", [128, 512], F32))
                    ps_vt = T2(nc.psum_tensor("ps_vt", [128, 1024], BF16))

                    xf_t = x_full.rearrange("(t p) d -> t p d", p=128)
                    p.phase = "p2"
                    p.barrier()
                    g0 = [("dve", cw_done), ("act", W("act"))]
                    xgate = {0: list(g0), 1: list(g0)}
                    dve_blk_prev = 0
                    projte_prev = 0
                    xn4gate = 0
                    vt_prev = 0
                    qbm_prev = 0
                    kbm_prev = 0
                    ksm_prev = 0
                    qsm_prev = 0
                    x2v_gate = 0
                    for blk in range(N // 512 if "p2" in PH else 0):
                        pg = blk % 2
                        xsem = "dsa" if pg == 0 else "dsb"
                        xb = x4[pg]
                        lxs = []
                        for tt in range(4):
                            t = blk * 4 + tt
                            p.add("sync", lambda e, t=t, tt=tt, xb=xb: e.dma_start(
                                out=xb[:, tt * D:(tt + 1) * D], in_=xf_t[t]),
                                waits=xgate[pg] if tt == 0 else [],
                                inc=xsem, inc_n=16)
                            lxs.append(W(xsem))
                        for tt in range(4):
                            p.add("dve", lambda e, tt=tt, xb=xb: e.reduce_sum(
                                st2x[:, tt:tt + 1], xb[:, tt * D:(tt + 1) * D],
                                axis=AX.X), waits=[(xsem, lxs[tt])], inc="dve")
                        rds = W("dve")
                        for tt in range(4):
                            p.add("act", lambda e, tt=tt, xb=xb: e.activation(
                                sqscr[:, :], xb[:, tt * D:(tt + 1) * D], AF.Square,
                                accum_out=st2x[:, 4 + tt:5 + tt]),
                                waits=[(xsem, lxs[tt])], inc="act")
                        sqs = W("act")
                        p.add("dve", lambda e: e.tensor_scalar(
                            st2x[:, 8:12], st2x[:, 0:4], -1.0 / D, None, ALU.mult),
                            waits=[("dve", rds)], inc="dve")
                        v1 = W("dve")
                        p.add("dve", lambda e: e.tensor_tensor(
                            st2x[:, 0:4], st2x[:, 8:12], st2x[:, 8:12], ALU.mult),
                            waits=[("dve", v1)], inc="dve")
                        v2 = W("dve")
                        p.add("dve", lambda e: e.tensor_scalar(
                            st2x[:, 12:16], st2x[:, 4:8], 1.0 / D, None, ALU.mult),
                            waits=[("act", sqs), ("dve", v2)], inc="dve")
                        v3 = W("dve")
                        p.add("dve", lambda e: e.tensor_tensor(
                            st2x[:, 12:16], st2x[:, 12:16], st2x[:, 0:4],
                            ALU.subtract), waits=[("dve", v3)], inc="dve")
                        v4 = W("dve")
                        p.add("act", lambda e: e.activation(
                            st2x[:, 16:20], st2x[:, 12:16], AF.Ln, bias=ceps[:, 0:1]),
                            waits=[("dve", v4)], inc="act")
                        a1 = W("act")
                        p.add("act", lambda e: e.activation(
                            st2x[:, 16:20], st2x[:, 16:20], AF.Exp,
                            scale=cnhalf[:, 0:1]), waits=[("act", a1)], inc="act")
                        a2 = W("act")
                        p.add("dve", lambda e: e.tensor_tensor(
                            st2x[:, 8:12], st2x[:, 8:12], st2x[:, 16:20], ALU.mult),
                            waits=[("act", a2)], inc="dve")
                        nmrs = W("dve")
                        for tt in range(4):
                            p.add("act", lambda e, tt=tt, xb=xb: e.activation(
                                xn4[:, tt * D:(tt + 1) * D],
                                xb[:, tt * D:(tt + 1) * D], AF.Identity,
                                bias=st2x[:, 8 + tt:9 + tt],
                                scale=st2x[:, 16 + tt:17 + tt]),
                                waits=[("dve", nmrs), ("act", xn4gate)], inc="act")
                        ap1 = W("act")
                        xgate[pg] = [("act", ap1)]
                        for tt in range(4):
                            p.add("dve", lambda e, tt=tt: e.reduce_sum(
                                st2x[:, tt:tt + 1], xn4[:, tt * D:(tt + 1) * D],
                                axis=AX.X), waits=[("act", ap1)], inc="dve")
                        rds2 = W("dve")
                        for tt in range(4):
                            p.add("act", lambda e, tt=tt: e.activation(
                                sqscr[:, :], xn4[:, tt * D:(tt + 1) * D], AF.Square,
                                accum_out=st2x[:, 4 + tt:5 + tt]),
                                waits=[("act", ap1)], inc="act")
                        sqs2 = W("act")
                        p.add("dve", lambda e: e.tensor_scalar(
                            st2x[:, 8:12], st2x[:, 0:4], -1.0 / D, None, ALU.mult),
                            waits=[("dve", rds2)], inc="dve")
                        v1 = W("dve")
                        p.add("dve", lambda e: e.tensor_tensor(
                            st2x[:, 0:4], st2x[:, 8:12], st2x[:, 8:12], ALU.mult),
                            waits=[("dve", v1)], inc="dve")
                        v2 = W("dve")
                        p.add("dve", lambda e: e.tensor_scalar(
                            st2x[:, 12:16], st2x[:, 4:8], 1.0 / D, None, ALU.mult),
                            waits=[("act", sqs2), ("dve", v2)], inc="dve")
                        v3 = W("dve")
                        p.add("dve", lambda e: e.tensor_tensor(
                            st2x[:, 12:16], st2x[:, 12:16], st2x[:, 0:4],
                            ALU.subtract), waits=[("dve", v3)], inc="dve")
                        v4 = W("dve")
                        p.add("act", lambda e: e.activation(
                            st2x[:, 16:20], st2x[:, 12:16], AF.Ln, bias=ceps[:, 0:1]),
                            waits=[("dve", v4)], inc="act")
                        a1 = W("act")
                        p.add("act", lambda e: e.activation(
                            st2x[:, 16:20], st2x[:, 16:20], AF.Exp,
                            scale=cnhalf[:, 0:1]), waits=[("act", a1)], inc="act")
                        a2 = W("act")
                        p.add("dve", lambda e: e.tensor_tensor(
                            st2x[:, 8:12], st2x[:, 8:12], st2x[:, 16:20], ALU.mult),
                            waits=[("act", a2)], inc="dve")
                        nmrs2 = W("dve")
                        for tt in range(4):
                            p.add("act", lambda e, tt=tt: e.activation(
                                xn2b[:, :], xn4[:, tt * D:(tt + 1) * D], AF.Identity,
                                bias=st2x[:, 8 + tt:9 + tt],
                                scale=st2x[:, 16 + tt:17 + tt]),
                                waits=[("dve", nmrs2), ("te", x2v_gate)], inc="act")
                            x2v = W("act")
                            for dc in range(DC):
                                p.add("te", lambda e, dc=dc: e.transpose(
                                    ps_tb[:, dc * 128:(dc + 1) * 128],
                                    xn2b[:, dc * 128:(dc + 1) * 128], eyeb[:, :]),
                                    waits=[("act", x2v)], inc="te")
                            trv = W("te")
                            x2v_gate = trv
                            p.add("act", lambda e, tt=tt: e.activation(
                                x2Tb[:, :].rearrange("p (c n) -> p c n", c=DC)
                                [:, :, tt * 128:(tt + 1) * 128],
                                ps_tb[:, :].rearrange("p (c n) -> p c n", c=DC),
                                AF.Copy),
                                waits=[("te", trv), ("te", projte_prev)], inc="act")
                        xn4gate = W("act")
                        blkv = W("act")

                        for dc in range(DC):
                            p.add("te", lambda e, dc=dc: e.matmul(
                                ps_q[:, :], wqT[:, dc * 128:(dc + 1) * 128],
                                x2Tb[:, dc * 512:(dc + 1) * 512],
                                start=(dc == 0), stop=(dc == DC - 1)),
                                waits=([("act", blkv), ("dve", dve_blk_prev)]
                                       if dc == 0 else []), inc="te")
                        qv = W("te")
                        for dc in range(DC):
                            p.add("te", lambda e, dc=dc: e.matmul(
                                ps_kv[:, :], wkvT[:, dc * 128:(dc + 1) * 128],
                                x2Tb[:, dc * 512:(dc + 1) * 512],
                                start=(dc == 0), stop=(dc == DC - 1)), inc="te")
                        kvv = W("te")
                        projte_prev = kvv
                        p.add("act", lambda e: e.activation(
                            kvsc[:, :], ps_kv[:, :], AF.Copy),
                            waits=[("te", kvv), ("te", vt_prev)], inc="act")
                        kvev = W("act")
                        p.add("act", lambda e: e.activation(
                            qksq[:, :], ps_q[:, :], AF.Square),
                            waits=[("te", qv), ("te", qbm_prev)], inc="act")
                        qsv = W("act")
                        p.add("dve", lambda e: e.tensor_copy(qksqh[:, :], qksq[:, :]),
                              waits=[("act", qsv), ("te", qsm_prev)], inc="dve")
                        qh1 = W("dve")
                        p.add("dve", lambda e: e.scalar_tensor_tensor(
                            qksql[:, :], qksqh[:, :], -1.0, qksq[:, :],
                            ALU.mult, ALU.add), waits=[("dve", qh1)], inc="dve")
                        qh2 = W("dve")
                        p.add("te", lambda e: e.matmul(
                            ps_s2[:, :], qssT[:, :], qksqh[:, :], start=True, stop=False),
                            waits=[("dve", qh2)], inc="te")
                        p.add("te", lambda e: e.matmul(
                            ps_s2[:, :], qssT[:, :], qksql[:, :], start=False, stop=True),
                            inc="te")
                        qsm = W("te")
                        p.add("act", lambda e: e.activation(
                            qksq[0:64, :], kvsc[0:64, :], AF.Square),
                            waits=[("act", kvev), ("te", qsm), ("dve", qh2)],
                            inc="act")
                        ksv = W("act")
                        p.add("dve", lambda e: e.tensor_copy(
                            qksqh[0:64, :], qksq[0:64, :]),
                            waits=[("act", ksv), ("te", qsm)], inc="dve")
                        kh1 = W("dve")
                        p.add("dve", lambda e: e.scalar_tensor_tensor(
                            qksql[0:64, :], qksqh[0:64, :], -1.0, qksq[0:64, :],
                            ALU.mult, ALU.add), waits=[("dve", kh1)], inc="dve")
                        kh2 = W("dve")
                        p.add("te", lambda e: e.matmul(
                            ps_s1[:, :], qssT[0:64, 0:1], qksqh[0:64, :],
                            start=True, stop=False), waits=[("dve", kh2)], inc="te")
                        p.add("te", lambda e: e.matmul(
                            ps_s1[:, :], qssT[0:64, 0:1], qksql[0:64, :],
                            start=False, stop=True), inc="te")
                        ksm = W("te")
                        ksm_prev = ksm
                        qsm_prev = qsm
                        p.add("dve", lambda e: e.tensor_scalar(
                            ss2[:, :], ps_s2[:, :], 1.0, 3.90625e-27,
                            ALU.mult, ALU.max),
                            waits=[("te", qsm), ("te", qbm_prev)], inc="dve")
                        qnv = W("dve")
                        p.add("act", lambda e: e.activation(ss2[:, :], ss2[:, :], AF.Ln),
                              waits=[("dve", qnv)], inc="act")
                        p.add("act", lambda e: e.activation(
                            ss2[:, :], ss2[:, :], AF.Exp, bias=cln4[0:2, 0:1],
                            scale=cnhalf[0:2, 0:1]),
                            waits=[("act", W("act"))], inc="act")
                        qfv = W("act")
                        p.add("dve", lambda e: e.tensor_scalar(
                            ss1[:, :], ps_s1[:, :], 1.0, 1e-24, ALU.mult, ALU.max),
                            waits=[("te", ksm), ("te", kbm_prev)], inc="dve")
                        knv = W("dve")
                        p.add("act", lambda e: e.activation(ss1[:, :], ss1[:, :], AF.Ln),
                              waits=[("dve", knv)], inc="act")
                        p.add("act", lambda e: e.activation(
                            ss1[:, :], ss1[:, :], AF.Exp, bias=cln4[0:1, 0:1],
                            scale=cnhalf[0:1, 0:1]),
                            waits=[("act", W("act"))], inc="act")
                        kfv = W("act")
                        p.add("dve", lambda e: e.tensor_copy(ssh[:, 0:512], ss2[:, :]),
                              waits=[("act", qfv), ("te", qbm_prev)], inc="dve")
                        qb1 = W("dve")
                        p.add("dve", lambda e: e.scalar_tensor_tensor(
                            ssl[:, 0:512], ssh[:, 0:512], -1.0, ss2[:, :],
                            ALU.mult, ALU.add), waits=[("dve", qb1)], inc="dve")
                        qb2 = W("dve")
                        p.add("te", lambda e: e.matmul(
                            ps_bc[:, :], bc2b[:, :], ssh[:, 0:512],
                            start=True, stop=False), waits=[("dve", qb2)], inc="te")
                        p.add("te", lambda e: e.matmul(
                            ps_bc[:, :], bc2b[:, :], ssl[:, 0:512],
                            start=False, stop=True), inc="te")
                        qbm = W("te")
                        qbm_prev = qbm
                        p.add("dve", lambda e: e.tensor_copy(rbq[:, :], ps_bc[:, :]),
                              waits=[("te", qbm)], inc="dve")
                        qbe = W("dve")
                        p.add("dve", lambda e, blk=blk: e.scalar_tensor_tensor(
                            qhT[:, blk * 512:(blk + 1) * 512], ps_q[:, :], 1.0,
                            rbq[:, :], ALU.mult, ALU.mult),
                            waits=[("dve", qbe)], inc="dve")
                        qhv = W("dve")
                        p.add("dve", lambda e: e.tensor_copy(ssh[0:1, 512:1024], ss1[:, :]),
                              waits=[("act", kfv), ("te", kbm_prev)], inc="dve")
                        kb1 = W("dve")
                        p.add("dve", lambda e: e.scalar_tensor_tensor(
                            ssl[0:1, 512:1024], ssh[0:1, 512:1024], -1.0, ss1[:, :],
                            ALU.mult, ALU.add), waits=[("dve", kb1)], inc="dve")
                        kb2 = W("dve")
                        p.add("te", lambda e: e.matmul(
                            ps_bc[0:64, :], bc2b[0:1, 0:64], ssh[0:1, 512:1024],
                            start=True, stop=False),
                            waits=[("dve", kb2), ("dve", qhv)], inc="te")
                        p.add("te", lambda e: e.matmul(
                            ps_bc[0:64, :], bc2b[0:1, 0:64], ssl[0:1, 512:1024],
                            start=False, stop=True), inc="te")
                        kbm = W("te")
                        kbm_prev = kbm
                        p.add("dve", lambda e: e.tensor_copy(rbk[:, :], ps_bc[0:64, :]),
                              waits=[("te", kbm)], inc="dve")
                        kbe = W("dve")
                        p.add("dve", lambda e, blk=blk: e.scalar_tensor_tensor(
                            kbuf[0:64, blk * 512:(blk + 1) * 512], ps_kv[0:64, :],
                            1.0, rbk[:, :], ALU.mult, ALU.mult),
                            waits=[("dve", kbe)], inc="dve")
                        khv = W("dve")
                        for cc4 in range(4):
                            g = blk * 4 + cc4
                            p.add("te", lambda e, cc4=cc4: e.transpose(
                                ps_vt[:, 0:DH], kvsc[64:128, cc4 * 128:(cc4 + 1) * 128],
                                eyeb[64:128, 64:128]),
                                waits=[("act", kvev), ("dve", W("dve"))], inc="te")
                            vtv = W("te")
                            p.add("dve", lambda e, g=g: e.tensor_copy(
                                vbuf[:, g * (DH + 1): g * (DH + 1) + DH], ps_vt[:, 0:DH]),
                                waits=[("te", vtv)], inc="dve")
                        vt_prev = W("te")
                        dve_blk_prev = W("dve")

                    if "vones" not in KSKIP:
                        p.add("dve", lambda e: e.memset(
                            vbuf[:, :].rearrange("p (c w) -> p c w", w=DH + 1)
                            [:, :, DH:DH + 1], 1.0), inc="dve")
                    prep_dve = W("dve")
                    if "kdup" not in KSKIP:
                        p.add("gps", lambda e: e.dma_start(
                            out=kbuf[64:128, :], in_=kbuf[0:64, :]),
                            waits=[("dve", prep_dve)], inc="dgp", inc_n=16)
                    kdup = W("dgp")

            p.phase = "p3"
            p.barrier()
            with ExitStack() as sc_at:
                Ta = sc_at.enter_context
                ptb = Ta(nc.sbuf_tensor("ptb", [128, 4 * 1024], BF16))
                rsum = Ta(nc.sbuf_tensor("rsum", [128, 512], F32))
                rshl = Ta(nc.sbuf_tensor("rshl", [128, 2 * 512], BF16))
                atile = Ta(nc.sbuf_tensor("atile", [64, 4 * 512], BF16))
                rbs = Ta(nc.sbuf_tensor("rbs", [64, 512], F32))
                ps_S = [Ta(nc.psum_tensor(f"ps_S{i}", [128, 1024], F32))
                        for i in range(B)]
                ps_O = [[Ta(nc.psum_tensor(f"ps_O{i}_{h}", [65, 512], F32))
                         for h in range(HPC)] for i in range(B)]

                mm1vals = [{} for _ in range(B)]
                expsem = [{} for _ in range(B)]
                mm2te = [{} for _ in range(B)]
                odiv = {(b, h): 0 for b in range(B) for h in range(HPC)}
                sdiv = {b: 0 for b in range(B)}
                aslot_gate = {i: ("dg%d" % i, 0) for i in range(4)}
                rbv_prev = 0
                rbe_prev = 0
                aidx = 0

                def emit_mm1(b, j, c):
                    klo = b * S + c * KC
                    qlo = b * S + j * QB
                    w = []
                    if c == 0:
                        w = [("dve", sdiv[b])]
                        if j == 0 and b == 0:
                            w += [("dgp", kdup), ("dve", prep_dve)]
                    elif c >= 1:
                        w = [expsem[b][c - 1]]
                    for hh in range(HPC):
                        p.add("te", lambda e, hh=hh, b=b, klo=klo, qlo=qlo: e.matmul(
                            ps_S[b][:, hh * 512:(hh + 1) * 512],
                            kbuf[hh * 64:hh * 64 + 64, klo:klo + KC],
                            qhT[hh * 64:hh * 64 + 64, qlo:qlo + QB],
                            start=True, stop=True),
                            waits=w if hh == 0 else [], inc="te")
                    mm1vals[b][c] = W("te")

                def emit_exp(b, j, c):
                    pp = b * 2 + (c % 2)
                    w = [("te", mm1vals[b][c])]
                    if c >= 2:
                        w.append(("te", mm2te[b][c - 2]))
                    p.add("act", lambda e, b=b, pp=pp: e.activation(
                        ptb[:, pp * 1024:(pp + 1) * 1024], ps_S[b][:, :], AF.Exp),
                        waits=w, inc="act")
                    r = c - RPQ * j
                    if 0 <= r < RPQ:
                        ev = W("act")
                        for hh in range(HPC):
                            p.add("dve", lambda e, pp=pp, r=r, hh=hh: e.tensor_tensor(
                                ptb[:, pp * 1024 + hh * 512: pp * 1024 + (hh + 1) * 512],
                                ptb[:, pp * 1024 + hh * 512: pp * 1024 + (hh + 1) * 512],
                                maskb[:, r * QB:(r + 1) * QB], ALU.mult),
                                waits=[("act", ev)] if hh == 0 else [], inc="dve")
                        expsem[b][c] = ("dve", W("dve"))
                    else:
                        expsem[b][c] = ("act", W("act"))

                def emit_mm2(b, j, c, nch):
                    pp = b * 2 + (c % 2)
                    g = (b * S) // KC + c
                    es, evv = expsem[b][c]
                    for hh in range(HPC):
                        w = [(es, evv)]
                        if c == 0:
                            w.append(("dve", odiv[(b, hh)]))
                        p.add("te", lambda e, hh=hh, b=b, pp=pp, g=g, c=c, nch=nch: e.matmul(
                            ps_O[b][hh][:, :],
                            vbuf[:, g * (DH + 1):(g + 1) * (DH + 1)],
                            ptb[:, pp * 1024 + hh * 512: pp * 1024 + (hh + 1) * 512],
                            start=(c == 0), stop=(c == nch - 1)),
                            waits=w if (hh == 0 or c == 0) else [], inc="te")
                    mm2te[b][c] = W("te")

                def emit_div(b, j, nch):
                    nonlocal rbv_prev, rbe_prev, aidx
                    qlo = b * S + j * QB
                    for hh in range(HPC):
                        p.add("dve", lambda e, b=b, hh=hh: e.reciprocal(
                            rsum[64:65, :], ps_O[b][hh][64:65, :]),
                            waits=[("te", mm2te[b][nch - 1]), ("te", rbv_prev)],
                            inc="dve")
                        rcv = W("dve")
                        p.add("dve", lambda e: e.tensor_copy(
                            rshl[64:65, 0:512], rsum[64:65, :]),
                            waits=[("dve", rcv)], inc="dve")
                        rh1 = W("dve")
                        p.add("dve", lambda e: e.scalar_tensor_tensor(
                            rshl[64:65, 512:1024], rshl[64:65, 0:512], -1.0,
                            rsum[64:65, :], ALU.mult, ALU.add),
                            waits=[("dve", rh1)], inc="dve")
                        rh2 = W("dve")
                        p.add("te", lambda e, b=b: e.matmul(
                            ps_S[b][0:64, 0:512], onesb[64:65, 0:64],
                            rshl[64:65, 0:512], start=True, stop=False),
                            waits=[("dve", rh2), ("dve", rbe_prev),
                                   expsem[b][nch - 1]], inc="te")
                        p.add("te", lambda e, b=b: e.matmul(
                            ps_S[b][0:64, 0:512], onesb[64:65, 0:64],
                            rshl[64:65, 512:1024], start=False, stop=True), inc="te")
                        rbv = W("te")
                        rbv_prev = rbv
                        p.add("dve", lambda e, b=b: e.tensor_copy(
                            rbs[:, :], ps_S[b][0:64, 0:512]),
                            waits=[("te", rbv)], inc="dve")
                        rbe = W("dve")
                        rbe_prev = rbe
                        slot = aidx % 4
                        p.add("dve", lambda e, b=b, hh=hh, slot=slot: e.scalar_tensor_tensor(
                            atile[:, slot * 512:(slot + 1) * 512],
                            ps_O[b][hh][0:64, :], 1.0, rbs[:, :], ALU.mult, ALU.mult),
                            waits=[("dve", rbe), aslot_gate[slot]], inc="dve")
                        mlv = W("dve")
                        odiv[(b, hh)] = mlv
                        asem = f"dg{slot}"
                        p.add("gps", lambda e, hh=hh, slot=slot, qlo=qlo: e.dma_start(
                            out=cc_in.ap()[hh * 64:(hh + 1) * 64, qlo:qlo + QB],
                            in_=atile[:, slot * 512:(slot + 1) * 512]),
                            waits=[("dve", mlv)], inc=asem, inc_n=16)
                        aslot_gate[slot] = (asem, W(asem))
                        aidx += 1
                    sdiv[b] = rbe_prev

                for j in range(NQB if "p3" in PH else 0):
                    nch = RPQ * (j + 1)
                    for c in range(nch):
                        for b in range(B):
                            emit_mm1(b, j, c)
                        for b in range(B):
                            emit_exp(b, j, c)
                        if c > 0:
                            for b in range(B):
                                emit_mm2(b, j, c - 1, nch)
                    for b in range(B):
                        emit_mm2(b, j, nch - 1, nch)
                    for b in range(B):
                        emit_div(b, j, nch)

                if "cc" in PH:
                    p.add("gps", lambda e: e.collective_compute(
                        "AllGather", ALU.bypass, replica_groups=[list(range(NCORES))],
                        ins=[cc_in.ap().opt()], outs=[cc_out.ap().opt()]),
                        waits=[("dgp", W("dgp")), ("dg0", W("dg0")),
                               ("dg1", W("dg1")), ("dg2", W("dg2")),
                               ("dg3", W("dg3"))], inc="cc", inc_n=None)
                if dbg:
                    bw = [("te", W("te")), ("act", W("act")), ("dve", W("dve"))]
                    p.add("gps", lambda e: e.dma_start(out=dbg_k, in_=kbuf[:, :]),
                          waits=bw, inc="dgp", inc_n=16)
                    p.add("gps", lambda e: e.dma_start(out=dbg_q, in_=qhT[:, :]),
                          inc="dgp", inc_n=16)
                    p.add("gps", lambda e: e.dma_start(out=dbg_v, in_=vbuf[:, :]),
                          inc="dgp", inc_n=16)
                    p.add("gps", lambda e: e.dma_start(out=dbg_xnT, in_=xnT[:, :]),
                          inc="dgp", inc_n=16)
                    p.add("gps", lambda e: e.dma_start(out=dbg_cw, in_=cwb[:, :]),
                          inc="dgp", inc_n=16)
                    p.add("gps", lambda e: e.dma_start(out=dbg_att, in_=cc_in.ap()),
                          waits=[("cc", 1)], inc="dgp", inc_n=16)
                    dbg_gate = W("dgp")

        p.phase = "p5"
        p.barrier()
        with ExitStack() as sc_f:
            Tf = sc_f.enter_context
            gacc = Tf(nc.sbuf_tensor("gacc", [128, DC * TOWN], F32))

            if "p5" not in PH:
                p.add("dve", lambda e: e.memset(gacc[:, :], 0.5), inc="dve")
            with ExitStack() as sc5:
                T5 = sc5.enter_context
                cwT = T5(nc.sbuf_tensor("cwT", [16, TOWN], BF16))
                cwbh = T5(nc.sbuf_tensor("cwbh", [128, 16], BF16))
                ohot = T5(nc.sbuf_tensor("ohot", [E, E * 128], BF16))
                p.add("sync", lambda e: e.dma_start(out=ohot[:, :], in_=ohot_i),
                      inc="dsy", inc_n=16)
                ohot_ld = W("dsy")
                hbuf = T5(nc.sbuf_tensor("hbuf", [128, HC * HTOK], BF16))
                wring = T5(nc.sbuf_tensor("wring", [128, 2 * 512], F32))
                w1b = [T5(nc.sbuf_tensor(f"w1b{i}", [128, DC * HF], BF16))
                       for i in range(2)]
                w2b = [T5(nc.sbuf_tensor(f"w2b{i}", [128, HC * D], BF16))
                       for i in range(2)]
                cwbc = T5(nc.sbuf_tensor("cwbc", [128, HTOK], F32))
                yscr = T5(nc.sbuf_tensor("yscr", [128, TB], F32))
                ps_cw = T5(nc.psum_tensor("ps_cw", [16, 1024], BF16))
                ps_H = [T5(nc.psum_tensor(f"ps_H{i}", [128, 512], F32))
                        for i in range(TGRP)]
                ps_Y = [T5(nc.psum_tensor(f"ps_Y{i}", [128, 512], F32))
                        for i in range(2)]
                ps_cb = [T5(nc.psum_tensor(f"ps_cb{i}", [128, 512], F32))
                         for i in range(2)]

                p.phase = "p5x"
                cwt_e_prev = 0
                cwt_te_prev = 0
                for t in range(OTILE if "p5" in PH else 0):
                    p.add("dve", lambda e, t=t: e.tensor_copy(
                        cwbh[:, :], cwb[:, t * 16:(t + 1) * 16]),
                        waits=[("dve", cw_done), ("te", cwt_te_prev)], inc="dve")
                    cv5 = W("dve")
                    p.add("te", lambda e, t=t: e.transpose(
                        ps_cw[:, 0:128], cwbh[:, :], eyeb[:, :]),
                        waits=[("dve", cv5), ("dve", cwt_e_prev)], inc="te")
                    tv = W("te")
                    cwt_te_prev = tv
                    p.add("dve", lambda e, t=t: e.tensor_copy(
                        cwT[0:E, t * 128:(t + 1) * 128], ps_cw[0:E, 0:128]),
                        waits=[("te", tv)], inc="dve")
                    cwt_e_prev = W("dve")
                cwtv = W("dve")

                p.phase = "p5"
                mats = [(sw1_i, "w1"), (sw2_i, "w2")]
                for ee in range(E):
                    mats.append((ew1_i[ee], "w1"))
                    mats.append((ew2_i[ee], "w2"))
                conv_done = {}
                ringlast = [0, 0]
                slot_eng = ["sync", "act"]
                slot_sem = ["wr0", "wr1"]
                wch = [0]
                feedq = []

                def queue_matrix(m, gate_te):
                    src, kindm = mats[m]
                    buf = (w1b if kindm == "w1" else w2b)[(m // 2) % 2]
                    for ch in range(2 * DC):
                        feedq.append((m, ch, src, buf, gate_te))

                def feed(n=1):
                    for _ in range(n):
                        if not feedq:
                            return
                        m, ch, src, buf, gate_te = feedq.pop(0)
                        ph_save = p.phase
                        p.phase = "p5w"
                        slot = wch[0] % 2
                        wch[0] += 1
                        wsem = slot_sem[slot]
                        r0 = (ch // 2) * 128
                        c0 = (ch % 2) * 512
                        gate = ([("dve", ringlast[slot])] if ringlast[slot] else [])
                        p.add(slot_eng[slot], lambda e, src=src, r0=r0, c0=c0, slot=slot: e.dma_start(
                            out=wring[:, slot * 512:(slot + 1) * 512],
                            in_=src[r0:r0 + 128, c0:c0 + 512]),
                            waits=gate, inc=wsem, inc_n=16)
                        lv = W(wsem)
                        p.add("dve", lambda e, r0=r0, c0=c0, slot=slot, buf=buf: e.tensor_copy(
                            buf[:, r0 * 8 + c0:r0 * 8 + c0 + 512],
                            wring[:, slot * 512:(slot + 1) * 512]),
                            waits=[(wsem, lv), ("te", gate_te)], inc="dve")
                        ringlast[slot] = W("dve")
                        conv_done[m] = W("dve")
                        p.phase = ph_save

                if "p5" in PH:
                    queue_matrix(0, 0)
                    queue_matrix(1, 0)
                    feed(len(feedq))

                cb_evac = {0: 0, 1: 0}
                hevac = {i: 0 for i in range(2 * TGRP)}
                yevac = {0: 0, 1: 0}
                yevac_a = {0: 0, 1: 0}
                gacc_last = {}
                hbuf_gate = 0

                te_end = {}
                for m in range(0, len(mats) if "p5" in PH else 0, 2):
                    ee = m // 2 - 1
                    pbuf = (m // 2) % 2
                    W1 = w1b[pbuf]
                    W2 = w2b[pbuf]
                    if m + 2 < len(mats):
                        g = W("te")
                        queue_matrix(m + 2, g)
                        queue_matrix(m + 3, g)
                    for grp in range(NGRP):
                        t0 = grp * HTOK
                        if ee >= 0:
                            for tb in range(TGRP):
                                cb = tb % 2
                                p.add("te", lambda e, ee=ee, t0=t0, tb=tb, cb=cb: e.matmul(
                                    ps_cb[cb][:, 0:TB], ohot[:, ee * 128:(ee + 1) * 128],
                                    cwT[0:E, t0 + tb * TB:t0 + (tb + 1) * TB],
                                    start=True, stop=True),
                                    waits=[("dve", cwtv), ("dsy", ohot_ld),
                                           ("dve", cb_evac[cb])], inc="te")
                                tv = W("te")
                                p.add("dve", lambda e, tb=tb, cb=cb: e.tensor_copy(
                                    cwbc[:, tb * TB:(tb + 1) * TB], ps_cb[cb][:, 0:TB]),
                                    waits=[("te", tv)], inc="dve")
                                cb_evac[cb] = W("dve")
                            cwbv = W("dve")
                        for hc in range(HC):
                            hp = 0
                            for dc in range(DC):
                                for tb in range(TGRP):
                                    w = []
                                    if dc == 0:
                                        w = [("dve", conv_done[m]),
                                             ("act", hevac[hp + tb])]
                                    p.add("te", lambda e, hc=hc, dc=dc, tb=tb,
                                          t0=t0, W1=W1, hp=hp: e.matmul(
                                        ps_H[hp + tb][:, 0:TB],
                                        W1[:, dc * HF + hc * 128:
                                           dc * HF + (hc + 1) * 128],
                                        xnT[:, dc * TOWN + t0 + tb * TB:
                                            dc * TOWN + t0 + (tb + 1) * TB],
                                        start=(dc == 0), stop=(dc == DC - 1)),
                                        waits=w, inc="te")
                            m1v = W("te")
                            for tb in range(TGRP):
                                bias = (sb1c[:, hc:hc + 1] if ee < 0
                                        else eb1c[:, ee * HC + hc:ee * HC + hc + 1])
                                p.add("act", lambda e, hc=hc, tb=tb, bias=bias,
                                      t0=t0, hp=hp: e.activation(
                                    hbuf[:, hc * HTOK + tb * TB:
                                         hc * HTOK + (tb + 1) * TB],
                                    ps_H[hp + tb][:, 0:TB], AF.Relu, bias=bias),
                                    waits=[("te", m1v), ("te", hbuf_gate)], inc="act")
                                hevac[hp + tb] = W("act")
                            feed(1)
                        hdone = W("act")
                        for dc2 in range(DC):
                            for tb in range(TGRP):
                                yb = (dc2 * TGRP + tb) % 2
                                for hc in range(HC):
                                    w = []
                                    if hc == 0:
                                        w = [("act", hdone), ("dve", yevac[yb]),
                                             ("act", yevac_a[yb]),
                                             ("dve", conv_done[m + 1])]
                                    p.add("te", lambda e, dc2=dc2, tb=tb, hc=hc,
                                          W2=W2, yb=yb, t0=t0, ee=ee: e.matmul(
                                        ps_Y[yb][:, 0:TB],
                                        W2[:, hc * D + dc2 * 128:
                                           hc * D + (dc2 + 1) * 128],
                                        hbuf[:, hc * HTOK + tb * TB:
                                             hc * HTOK + (tb + 1) * TB],
                                        start=(hc == 0),
                                        stop=(hc == HC - 1 and ee >= 0)),
                                        waits=w, inc="te")
                                if ee < 0:
                                    p.add("te", lambda e, dc2=dc2, tb=tb, t0=t0,
                                          yb=yb: e.matmul(
                                        ps_Y[yb][:, 0:TB],
                                        eb2t[:, dc2 * 128:(dc2 + 1) * 128],
                                        cwT[0:E, t0 + tb * TB: t0 + (tb + 1) * TB],
                                        start=False, stop=True),
                                        waits=[("dve", cwtv)], inc="te")
                                m2v = W("te")
                                gsl = gacc[:, dc2 * TOWN + t0 + tb * TB:
                                           dc2 * TOWN + t0 + (tb + 1) * TB]
                                gkey = (dc2, grp, tb)
                                if ee < 0:
                                    p.add("act", lambda e, gsl=gsl, dc2=dc2,
                                          yb=yb: e.activation(
                                        gsl, ps_Y[yb][:, 0:TB], AF.Identity,
                                        bias=sb2c[:, dc2:dc2 + 1]),
                                        waits=[("te", m2v)], inc="act")
                                    yevac_a[yb] = W("act")
                                    gacc_last[gkey] = ("act", W("act"))
                                    feed(1)
                                else:
                                    p.add("dve", lambda e, tb=tb, yb=yb: e.scalar_tensor_tensor(
                                        yscr[:, :], ps_Y[yb][:, 0:TB], 1.0,
                                        cwbc[:, tb * TB:(tb + 1) * TB],
                                        ALU.mult, ALU.mult),
                                        waits=[("te", m2v), ("dve", cwbv)], inc="dve")
                                    sv = W("dve")
                                    gw = [("dve", sv)]
                                    if gkey in gacc_last:
                                        gw.append(gacc_last[gkey])
                                    p.add("dve", lambda e, gsl=gsl: e.tensor_tensor(
                                        gsl, gsl, yscr[:, :], ALU.add),
                                        waits=gw, inc="dve")
                                    yevac[yb] = W("dve")
                                    gacc_last[gkey] = ("dve", W("dve"))
                                feed(1)
                        hbuf_gate = W("te")
                    feed(len(feedq))
                    te_end[m // 2] = W("te")
                ffn_act = W("act")
                ffn_dve = W("dve")

            p.phase = "p6"
            p.barrier()
            with ExitStack() as sc6:
                T6 = sc6.enter_context
                afull = T6(nc.sbuf_tensor("afull", [128, DC * TOWN], BF16))
                o2scr = T6(nc.sbuf_tensor("o2scr", [128, DC * TB], BF16))
                o2t = T6(nc.sbuf_tensor("o2t", [128, D], F32))
                st6 = T6(nc.sbuf_tensor("st6", [128, 8], F32))
                stscr6 = T6(nc.sbuf_tensor("stscr6", [128, (TOWN // 128) * 8], F32))
                if dbg:
                    p.add("dve", lambda e: e.memset(stscr6[:, :], 0.0), inc="dve")
                woutT = T6(nc.sbuf_tensor("woutT", [128, DC * D], BF16))
                w6f = T6(nc.sbuf_tensor("w6f", [128, D], F32))
                w6b = T6(nc.sbuf_tensor("w6b", [128, D], BF16))
                g2bc = T6(nc.sbuf_tensor("g2bc", [128, D], F32))
                g2row = T6(nc.sbuf_tensor("g2row", [1, D], F32))
                g2hl = T6(nc.sbuf_tensor("g2hl", [1, 2 * D], BF16))
                stile = T6(nc.sbuf_tensor("stile", [128, 2 * D], BF16))
                ps_o2 = [T6(nc.psum_tensor(f"ps_o2{i}", [128, 512], F32))
                         for i in range(2)]
                ps_t6 = T6(nc.psum_tensor("ps_t6", [128, 1024], BF16))
                ps_w6 = T6(nc.psum_tensor("ps_w6", [128, 1024], BF16))

                if "g2w" in KSKIP:
                    p.add("dve", lambda e: e.memset(g2bc[:, :], 1.0), inc="dve")
                p.add("sync", lambda e: e.dma_start(out=g2row[:, :], in_=g2_r),
                      inc="dsy", inc_n=16) if "g2w" not in KSKIP else None
                g2ld = W("dsy")
                g2evac = 0
                if "g2w" not in KSKIP:
                    p.add("dve", lambda e: e.tensor_copy(g2hl[0:1, 0:D], g2row[0:1, :]),
                          waits=[("dsy", g2ld)], inc="dve")
                    gh1 = W("dve")
                    p.add("dve", lambda e: e.scalar_tensor_tensor(
                        g2hl[0:1, D:2 * D], g2hl[0:1, 0:D], -1.0, g2row[0:1, :],
                        ALU.mult, ALU.add), waits=[("dve", gh1)], inc="dve")
                    g2evac = W("dve")
                for h in range(D // 512 if "g2w" not in KSKIP else 0):
                    p.add("te", lambda e, h=h: e.matmul(
                        ps_o2[0][:, :], onesb[0:1, :],
                        g2hl[0:1, h * 512:(h + 1) * 512],
                        start=True, stop=False),
                        waits=[("dve", g2evac)], inc="te")
                    p.add("te", lambda e, h=h: e.matmul(
                        ps_o2[0][:, :], onesb[0:1, :],
                        g2hl[0:1, D + h * 512:D + (h + 1) * 512],
                        start=False, stop=True), inc="te")
                    tv = W("te")
                    p.add("dve", lambda e, h=h: e.tensor_copy(
                        g2bc[:, h * 512:(h + 1) * 512], ps_o2[0][:, :]),
                        waits=[("te", tv)], inc="dve")
                    g2evac = W("dve")
                g2v = W("dve")
                wt_gate_d = g2v
                wt_gate_a = 0
                for mc in range(DC if "p6" in PH else 0):
                    p.add("sync", lambda e, mc=mc: e.dma_start(
                        out=w6f[:, :], in_=wout_i[mc * 128:(mc + 1) * 128, :]),
                        waits=[("dve", wt_gate_d), ("act", wt_gate_a)],
                        inc="dsy", inc_n=16)
                    lv = W("dsy")
                    p.add("dve", lambda e: e.tensor_copy(w6b[:, :], w6f[:, :]),
                          waits=[("dsy", lv), ("te", W("te"))], inc="dve")
                    cv = W("dve")
                    wt_gate_d = cv
                    for ic in range(DC):
                        p.add("te", lambda e, ic=ic: e.transpose(
                            ps_w6[:, ic * 128:(ic + 1) * 128],
                            w6b[:, ic * 128:(ic + 1) * 128], eyeb[:, :]),
                            waits=([("dve", cv), ("act", wt_gate_a)]
                                   if ic == 0 else []), inc="te")
                    tv = W("te")
                    p.add("act", lambda e, mc=mc: e.activation(
                        woutT[:, :].rearrange("p (c n) -> p c n", c=DC)
                        [:, :, mc * 128:(mc + 1) * 128],
                        ps_w6[:, :].rearrange("p (c n) -> p c n", c=DC),
                        AF.Copy), waits=[("te", tv)], inc="act")
                    wt_gate_a = W("act")
                wt_done = W("act")

                def rd_attfull(e, ic):
                    rk = e.cc_rank(replica_groups=[list(range(NCORES))])
                    co = cc_out.ap().rearrange("(r p) n -> r p n", p=128)
                    return e.dma_start(out=afull[:, ic * TOWN:(ic + 1) * TOWN],
                                       in_=co[ic, :, bass.ts(rk, TOWN)])
                for ic in range(DC if "p6" in PH else 0):
                    p.add("gps", lambda e, ic=ic: rd_attfull(e, ic),
                          waits=[("cc", 1) if "cc" in PH else ("cc", 0)],
                          inc="dgp", inc_n=16)
                afv = W("dgp")
                if dbg:
                    p.add("gps", lambda e: e.dma_start(out=dbg_afull, in_=afull[:, :]),
                          waits=[("dgp", afv)], inc="dgp", inc_n=16)

                evs = {0: 0, 1: 0}
                tr6_prev = 0
                s_done = 0
                sgate = {0: ("dya", 0), 1: ("dyb", 0)}
                o2rd_d = 0
                o2rd_a = 0
                ogate = 0
                for tbk in range(NB6 if "p6" in PH else 0):
                    t0 = tbk * TB
                    for mc in range(DC):
                        yb = mc % 2
                        for ic in range(DC):
                            w = []
                            if ic == 0:
                                w = [("dgp", afv), ("dve", evs[yb]),
                                     ("act", wt_done), ("dve", ffn_dve)]
                            p.add("te", lambda e, mc=mc, ic=ic, t0=t0, yb=yb: e.matmul(
                                ps_o2[yb][:, 0:TB],
                                woutT[:, ic * D + mc * 128: ic * D + (mc + 1) * 128],
                                afull[:, ic * TOWN + t0: ic * TOWN + t0 + TB],
                                start=(ic == 0), stop=(ic == DC - 1)),
                                waits=w, inc="te")
                        wv = W("te")
                        p.add("dve", lambda e, mc=mc, yb=yb: e.tensor_copy(
                            o2scr[:, mc * TB:(mc + 1) * TB], ps_o2[yb][:, 0:TB]),
                            waits=[("te", wv), ("te", tr6_prev)], inc="dve")
                        evs[yb] = W("dve")
                    o2v = W("dve")
                    if dbg and tbk == 0:
                        p.add("gps", lambda e: e.dma_start(
                            out=dbg_o2[:, 0:DC * TB], in_=o2scr[:, :]),
                            waits=[("dve", o2v)], inc="dgp", inc_n=16)
                        p.add("gps", lambda e: e.dma_start(
                            out=dbg_g2bc, in_=g2bc[:, :]),
                            waits=[("dve", g2v)], inc="dgp", inc_n=16)
                    for tt in range(TB // 128):
                        gt = (t0 + tt * 128) // 128
                        for mc in range(DC):
                            p.add("te", lambda e, mc=mc, tt=tt: e.transpose(
                                ps_t6[:, 0:128],
                                o2scr[:, mc * TB + tt * 128: mc * TB + (tt + 1) * 128],
                                eyeb[:, :]),
                                waits=[("dve", o2v), ("dve", ogate)], inc="te")
                            tv = W("te")
                            p.add("dve", lambda e, mc=mc: e.tensor_copy(
                                o2t[:, mc * 128:(mc + 1) * 128], ps_t6[:, 0:128]),
                                waits=[("te", tv), ("dve", o2rd_d), ("act", o2rd_a)],
                                inc="dve")
                            ogate = W("dve")
                        tr6_prev = W("te")
                        ov = W("dve")
                        if dbg and tbk == 0:
                            p.add("gps", lambda e, tt=tt: e.dma_start(
                                out=dbg_o2t[:, tt * D:(tt + 1) * D], in_=o2t[:, :]),
                                waits=[("dve", ov)], inc="dgp", inc_n=16)
                            o2t_dumped = W("dgp")
                            rdy, _ = ln_chain(o2t[:, :], st6, 0,
                                              [("dve", ov), ("dgp", o2t_dumped)])
                        else:
                            rdy, _ = ln_chain(o2t[:, :], st6, 0, [("dve", ov)])
                        p.add("dve", lambda e: e.tensor_scalar(
                            o2t[:, :], o2t[:, :], st6[:, 2:3], st6[:, 4:5],
                            ALU.add, ALU.mult), waits=[("act", rdy)], inc="dve")
                        lv = W("dve")
                        if dbg:
                            p.add("dve", lambda e, gt=gt: e.tensor_copy(
                                stscr6[:, gt * 8:gt * 8 + 5], st6[:, 0:5]),
                                waits=[("dve", lv)], inc="dve")
                        spg = gt % 2
                        p.add("dve", lambda e, spg=spg: e.tensor_tensor(
                            stile[:, spg * D:(spg + 1) * D], o2t[:, :], g2bc[:, :],
                            ALU.mult),
                            waits=[("dve", lv), sgate[spg]], inc="dve")
                        sv6 = W("dve")
                        o2rd_d = sv6
                        o2rd_a = W("act")
                        ssem = "dya" if spg == 0 else "dyb"
                        p.add("gps", lambda e, gt=gt, spg=spg: e.dma_start(
                            out=s_dram.ap()[gt], in_=stile[:, spg * D:(spg + 1) * D]),
                            waits=[("dve", sv6)], inc=ssem, inc_n=16)
                        sgate[spg] = (ssem, W(ssem))
                        s_done = W("dve")
                if dbg:
                    p.add("gps", lambda e: e.dma_start(out=dbg_st, in_=stscr6[:, :]),
                          waits=[("dve", W("dve"))], inc="dgp", inc_n=16)

            p.phase = "p7"
            p.barrier()
            with ExitStack() as sc7:
                T7 = sc7.enter_context
                gsil = T7(nc.sbuf_tensor("gsil", [128, DC * TOWN], BF16))
                gtile = T7(nc.sbuf_tensor("gtile", [128, D], BF16))
                xtl = T7(nc.sbuf_tensor("xtl", [128, 2 * D], F32))
                ytl = T7(nc.sbuf_tensor("ytl", [128, 2 * D], F32))
                st7 = T7(nc.sbuf_tensor("st7", [128, 2 * D], BF16))
                ps_g = T7(nc.psum_tensor("ps_g", [128, 1024], BF16))

                if dbg:
                    p.add("gps", lambda e: e.dma_start(out=dbg_g, in_=gacc[:, :]),
                          waits=[("dve", ffn_dve), ("act", ffn_act),
                                 ("dve", s_done)], inc="dgp", inc_n=16)
                    p.add("gps", lambda e: e.dma_start(
                        out=dbg_s.rearrange("p (t d) -> t p d", d=D),
                        in_=s_dram.ap()[:, :, :]), inc="dgp", inc_n=16)
                if "silu" in KSKIP:
                    for dc in range(DC):
                        p.add("dve", lambda e, dc=dc: e.tensor_copy(
                            gsil[:, dc * TOWN:(dc + 1) * TOWN],
                            gacc[:, dc * TOWN:(dc + 1) * TOWN]),
                            waits=[("dve", ffn_dve), ("act", ffn_act)], inc="dve")
                    silv = W("act")
                else:
                    for dc in range(DC):
                        p.add("act", lambda e, dc=dc: e.activation(
                            gsil[:, dc * TOWN:(dc + 1) * TOWN],
                            gacc[:, dc * TOWN:(dc + 1) * TOWN], AF.Silu),
                            waits=[("dve", ffn_dve), ("act", ffn_act)], inc="act")
                    silv = W("act")

                xo_t = x_own.rearrange("(t p) d -> t p d", p=128)
                yo_t = y_own.rearrange("(t p) d -> t p d", p=128)
                xread = {0: 0, 1: 0}
                ystore = {0: 0, 1: 0}
                ggate = 0
                ylast = 0
                for t in range(OTILE):
                    pg = t % 2
                    xsem = "dsa" if pg == 0 else "dsb"
                    lsem = "ds0" if pg == 0 else "ds1"
                    p.add("sync", lambda e, t=t, pg=pg: e.dma_start(
                        out=xtl[:, pg * D:(pg + 1) * D], in_=xo_t[t]),
                        waits=[("dve", xread[pg])], inc=xsem, inc_n=16)
                    xv7 = W(xsem)
                    if "p6" in PH:
                        p.add("sync", lambda e, t=t, pg=pg: e.dma_start(
                            out=st7[:, pg * D:(pg + 1) * D], in_=s_dram.ap()[t]),
                            inc=lsem, inc_n=16)
                    else:
                        p.add("dve", lambda e, pg=pg: e.memset(
                            st7[:, pg * D:(pg + 1) * D], 1.0), inc="dve")
                    lx7 = W(lsem)
                    for dc in range(DC):
                        p.add("te", lambda e, dc=dc, t=t: e.transpose(
                            ps_g[:, 0:128],
                            gsil[:, dc * TOWN + t * 128: dc * TOWN + (t + 1) * 128],
                            eyeb[:, :]),
                            waits=[("act", silv), ("dve", ggate)], inc="te")
                        tv = W("te")
                        p.add("dve", lambda e, dc=dc: e.tensor_copy(
                            gtile[:, dc * 128:(dc + 1) * 128], ps_g[:, 0:128]),
                            waits=[("te", tv)], inc="dve")
                        ggate = W("dve")
                    gv = W("dve")
                    ysem = "dya" if pg == 0 else "dyb"
                    p.add("dve", lambda e, t=t, pg=pg: e.tensor_tensor(
                        ytl[:, pg * D:(pg + 1) * D], st7[:, pg * D:(pg + 1) * D],
                        gtile[:, :], ALU.mult),
                        waits=[("dve", gv), (lsem, lx7), (ysem, ystore[pg])],
                        inc="dve")
                    m1 = W("dve")
                    p.add("dve", lambda e, t=t, pg=pg: e.tensor_tensor(
                        ytl[:, pg * D:(pg + 1) * D], ytl[:, pg * D:(pg + 1) * D],
                        xtl[:, pg * D:(pg + 1) * D], ALU.add),
                        waits=[("dve", m1), (xsem, xv7)], inc="dve")
                    yv = W("dve")
                    xread[pg] = yv
                    p.add("gps", lambda e, t=t, pg=pg: e.dma_start(
                        out=yo_t[t], in_=ytl[:, pg * D:(pg + 1) * D]),
                        waits=[("dve", yv)], inc=ysem, inc_n=16)
                    ystore[pg] = W(ysem)
                p.add("gps", lambda e: e.nop(),
                      waits=[("dya", W("dya")), ("dyb", W("dyb"))])

        with ExitStack() as es:
            sems = {s: es.enter_context(nc.semaphore(s)) for s in SEMS}
            blk = es.enter_context(nc.Block())
            blk.sync(lambda e: p.emit("sync", e, sems))
            blk.gpsimd(lambda e: p.emit("gps", e, sems))
            blk.tensor(lambda e: p.emit("te", e, sems))
            blk.scalar(lambda e: p.emit("act", e, sems))
            blk.vector(lambda e: p.emit("dve", e, sems))
        pre.close()

    return nc



def make_consts():
    eye_b = np.eye(128, dtype=ml_dtypes.bfloat16)
    eye_f = np.eye(128, dtype=np.float32)
    ones_f = np.ones((128, 128), dtype=np.float32)
    qss = np.zeros((128, 2), dtype=np.float32)
    qss[0:64, 0] = 1.0
    qss[64:128, 1] = 1.0
    qss = qss.astype(ml_dtypes.bfloat16)
    bc2 = np.zeros((2, 128), dtype=np.float32)
    bc2[0, 0:64] = 1.0
    bc2[1, 64:128] = 1.0
    ohot = np.zeros((E, E * 128), dtype=np.float32)
    for ee in range(E):
        ohot[ee, ee * 128:(ee + 1) * 128] = 1.0
    masks = np.zeros((128, RPQ * QB), dtype=np.float32)
    for r in range(RPQ):
        kp = np.arange(128)[:, None]
        ql = np.arange(QB)[None, :]
        masks[:, r * QB:(r + 1) * QB] = (kp + 128 * r <= ql).astype(np.float32)
    return (eye_b, eye_f, ones_f, qss, bc2, ohot.astype(ml_dtypes.bfloat16),
            masks.astype(ml_dtypes.bfloat16))


def make_in_maps(S, args):
    (x, g1, Wq, Wkv, Wout, g2, rW, rb, sW1, sb1, sW2, sb2,
     eW1, eb1, eW2, eb2) = args
    B = 2
    N = B * S
    TOWN = N // NCORES
    f32 = lambda a: np.ascontiguousarray(a, dtype=np.float32)
    xf = f32(x).reshape(N, D)
    eye_b, eye_f, ones_f, qss, bc2, ohot, masks = make_consts()
    g1c = f32(g1).reshape(DC, 128).T.copy()
    sb1c = f32(sb1).reshape(HC, 128).T.copy()
    sb2c = f32(sb2).reshape(DC, 128).T.copy()
    eb1c = f32(eb1).reshape(E, HC, 128).transpose(2, 0, 1).reshape(128, E * HC).copy()
    rwc = f32(rW).reshape(DC, 128, E).transpose(1, 0, 2).reshape(128, DC * E).copy()
    shared = {
        "x_full": xf,
        "wkv": f32(Wkv), "wout": f32(Wout),
        "g1_c": g1c, "g2_r": f32(g2).reshape(1, D),
        "rw_c": rwc, "rb_r": f32(rb).reshape(1, E),
        "sw1": f32(sW1), "sw2": f32(sW2),
        "sb1_c": sb1c, "sb2_c": sb2c,
        "ew1": f32(eW1), "ew2": f32(eW2),
        "eb1_c": eb1c, "eb2_t": f32(eb2),
        "eye_b": eye_b, "eye_f": eye_f, "ones_f": ones_f,
        "qss_t": qss, "bc2_t": bc2, "ohot_i": ohot, "masks": masks,
    }
    in_maps = []
    for c in range(NCORES):
        m = dict(shared)
        m["x_own"] = xf[c * TOWN:(c + 1) * TOWN].copy()
        m["wq_my"] = f32(Wq)[c * 128:(c + 1) * 128, :].copy()
        in_maps.append(m)
    return in_maps


def kernel(L0_attn_x, L0_attn_g1, L0_attn_Wq, L0_attn_Wkv, L0_attn_Wout,
           L0_attn_g2, L0_ffn_router_W, L0_ffn_router_b, L0_ffn_sW1, L0_ffn_sb1,
           L0_ffn_sW2, L0_ffn_sb2, L0_ffn_eW1, L0_ffn_eb1, L0_ffn_eW2,
           L0_ffn_eb2):
    x = np.asarray(L0_attn_x)
    B_, S_, D_ = x.shape
    assert B_ == 2 and D_ == D
    args = (x, L0_attn_g1, L0_attn_Wq, L0_attn_Wkv, L0_attn_Wout, L0_attn_g2,
            L0_ffn_router_W, L0_ffn_router_b, L0_ffn_sW1, L0_ffn_sb1,
            L0_ffn_sW2, L0_ffn_sb2, L0_ffn_eW1, L0_ffn_eb1, L0_ffn_eW2,
            L0_ffn_eb2)
    nc = build_nc(S_)
    in_maps = make_in_maps(S_, [np.asarray(a) for a in args])
    res = run_bass_kernel_spmd(nc, in_maps, core_ids=list(range(NCORES)))
    y = np.concatenate([res.results[c]["y_own"] for c in range(NCORES)], axis=0)
    return y.reshape(B_, S_, D_).astype(np.float32)
